# version 31
# baseline (speedup 1.0000x reference)
import math
import numpy as np
import ml_dtypes
from contextlib import ExitStack

import concourse.bass as bass
import concourse.mybir as mybir
from concourse.bass_utils import run_bass_kernel_spmd

F32 = mybir.dt.float32
BF16 = mybir.dt.bfloat16
AF = mybir.ActivationFunctionType
ALU = mybir.AluOpType
AX = mybir.AxisListType
BF = ml_dtypes.bfloat16

T = 2048
D = 2048
NH = 16
DH = 128
HF = 8192
NC = 16
NJ = 64
SCALE = 1.0 / math.sqrt(DH)
WHT_SCALE = 1.0 / math.sqrt(HF)
EPS = 1e-5

ENGINES = ("sync", "gpsimd", "tensor", "scalar", "vector")


class Sem:
    def __init__(self, stack, nc, name):
        self.h = stack.enter_context(nc.semaphore(name))
        self.name = name
        self.n = 0

    def inc(self, k=1):
        self.n += k
        return self.n


class B:

    def __init__(self):
        self.ops = {e: [] for e in ENGINES}
        self.last_wait = {}

    def emit(self, eng, fn):
        self.ops[eng].append(fn)

    def wait(self, eng, sem, val):
        if sem is None or val <= 0:
            return
        key = (eng, sem.name)
        if self.last_wait.get(key, 0) >= val:
            return
        self.last_wait[key] = val
        self.emit(eng, lambda e, s=sem.h, v=val: e.wait_ge(s, v))

    def waitp(self, eng, pair):
        self.wait(eng, pair[0], pair[1])

    def dma(self, eng, sem, fn):
        self.emit(eng, lambda e, fn=fn, s=sem.h: fn(e).then_inc(s, 16))
        return sem.inc(16)


def build(debug=False):
    nc = bass.Bass("TRN2", target_bir_lowering=False)
    SCR = "ExternalOutput" if debug else "Internal"

    x_d = nc.dram_tensor("x", [T, D], F32, kind="ExternalInput").ap()
    wqkv_d = nc.dram_tensor("w_qkv", [D, 3 * D], F32, kind="ExternalInput").ap()
    wout_d = nc.dram_tensor("w_out", [D, D], F32, kind="ExternalInput").ap()
    bout_d = nc.dram_tensor("b_out", [1, D], F32, kind="ExternalInput").ap()
    w1_d = nc.dram_tensor("W1", [HF, D], F32, kind="ExternalInput").ap()
    w2_d = nc.dram_tensor("W2", [D, HF], F32, kind="ExternalInput").ap()
    w3_d = nc.dram_tensor("W3", [HF, D], F32, kind="ExternalInput").ap()
    i128_d = nc.dram_tensor("i128", [128, 128], BF16, kind="ExternalInput").ap()
    h128_d = nc.dram_tensor("h128", [128, 128], BF16, kind="ExternalInput").ap()
    maskt_d = nc.dram_tensor("maskt", [128, 2048], BF16, kind="ExternalInput").ap()
    onesb_d = nc.dram_tensor("onesb", [128, 128], BF16, kind="ExternalInput").ap()
    onesf_d = nc.dram_tensor("onesf", [1, 128], F32, kind="ExternalInput").ap()
    y_d = nc.dram_tensor("y", [T, D], F32, kind="ExternalOutput").ap()
    w1b_d = nc.dram_tensor("w1b", [HF, D], BF16, kind="Internal").ap()
    w3b_d = nc.dram_tensor("w3b", [HF, D], BF16, kind="Internal").ap()
    w2b_d = nc.dram_tensor("w2b", [D, HF], BF16, kind="Internal").ap()
    w1t_d = nc.dram_tensor("w1t", [NJ, 128, D], BF16, kind="Internal").ap()
    w3t_d = nc.dram_tensor("w3t", [NJ, 128, D], BF16, kind="Internal").ap()
    w2t_d = nc.dram_tensor("w2t", [NJ, 128, D], BF16, kind="Internal").ap()
    vsc_d = nc.dram_tensor("vsc", [NC, 128, D], BF16, kind=SCR).ap()
    att_d = nc.dram_tensor("att", [NH, 128, T], BF16, kind=SCR).ap()
    x1sc_d = nc.dram_tensor("x1sc", [NC, 128, D], F32, kind=SCR).ap()
    dbg_y1_d = dbg_hid_d = None
    if debug:
        dbg_y1_d = nc.dram_tensor("dbg_y1", [NC, 128, D], F32, kind="ExternalOutput").ap()
        dbg_hid_d = nc.dram_tensor("dbg_hid", [128, NJ * 512], BF16, kind="ExternalOutput").ap()

    with ExitStack() as stack:
        def sb(name, cols, dtype=BF16, parts=128):
            return stack.enter_context(nc.sbuf_tensor(name, [parts, cols], dtype))

        bigA = sb("bigA", 16 * 2048)
        bigB = sb("bigB", 8192)
        qk = sb("qk", 4096)
        pt = sb("pt", 16 * 1024)
        outt = sb("outt", 2 * 2048)
        vsl = sb("vsl", 2 * 2048)
        warena = sb("warena", 8192)
        btmp = sb("btmp", 2048)
        sg = sb("sg", 2 * 512)
        xbt = sb("xbt", 2048)
        i128 = sb("i128s", 128)
        h128 = sb("h128s", 128)
        maskt = sb("maskts", 2048)
        ones128 = sb("ones128s", 128)
        onesf = sb("onesfs", 128, F32, parts=1)
        b_sb = sb("b_sb", 2048, F32, parts=1)
        b_bc = sb("b_bc", 2048, F32)
        fA = sb("fA", 2 * 2048, F32)
        fAb = fA.bitcast(BF16)
        fB = sb("fB", 2048, F32)
        st = sb("st", 16, F32)

        pbA = stack.enter_context(nc.psum_tensor("pbA", [128, 2048], F32))
        pbB = stack.enter_context(nc.psum_tensor("pbB", [128, 2048], F32))

        S = lambda name: Sem(stack, nc, name)
        s_cn = S("s_cn")
        g_xb = [S("g_xb0"), S("g_xb1")]
        g_c = {"w1": S("g_c1"), "w3": S("g_c3"), "w2": S("g_c2")}
        g_wv = S("g_wv")
        g_qk = [S("g_qk0"), S("g_qk1")]
        g_wo = S("g_wo")
        s_tr = S("s_tr")
        t_ps = S("t_ps")
        s_qtr = [S("s_qtr0"), S("s_qtr1")]
        t_qst = [S("t_qst0"), S("t_qst1")]
        s_q2r = [S("s_q2r0"), S("s_q2r1")]
        t_q2s = [S("t_q2s0"), S("t_q2s1")]
        t_bt = [S("t_bt0"), S("t_bt1"), S("t_bt2"), S("t_bt3")]
        s_vs = [S("s_vs0"), S("s_vs1")]
        t_ot = [S("t_ot0"), S("t_ot1")]
        s_fa = [S("s_fa0"), S("s_fa1")]
        t_fb = S("t_fb")
        s_w13 = [S("s_w13a"), S("s_w13b")]
        s_w2 = [S("s_w2%d" % i) for i in range(8)]
        t_dbg = S("t_dbg")
        pe = S("pe")
        act = S("act")
        dve = S("dve")

        b = B()
        NONE = (None, 0)

        MX, BIAS, SUM, RC, S1, M, NM, QS, M2, VAR, SD, RS = range(12)

        def stc(i):
            return st[:, i:i + 1]

        def emit_ln(buf=None):
            fBs = fB[:, 0:2048] if buf is None else buf
            def vstat(fn):
                b.emit("vector", lambda e, fn=fn: fn(e).then_inc(dve.h, 1))
                b.wait("vector", dve, dve.inc())
            vstat(lambda e: e.reduce_sum(stc(S1), fBs, axis=AX.X))
            vstat(lambda e: e.tensor_scalar_mul(stc(M), stc(S1), 1.0 / 2048.0))
            vstat(lambda e: e.tensor_mul(btmp[:, 0:2048], fBs, fBs))
            vstat(lambda e: e.reduce_sum(stc(QS), btmp[:, 0:2048], axis=AX.X))
            vstat(lambda e: e.tensor_mul(stc(M2), stc(M), stc(M)))
            vstat(lambda e: e.tensor_scalar_mul(stc(QS), stc(QS), 1.0 / 2048.0))
            vstat(lambda e: e.tensor_sub(stc(VAR), stc(QS), stc(M2)))
            vstat(lambda e: e.tensor_scalar_add(stc(VAR), stc(VAR), EPS))
            b.wait("scalar", dve, dve.n)
            b.emit("scalar", lambda e: e.sqrt(stc(SD), stc(VAR)).then_inc(act.h, 1))
            sd_act = act.inc()
            b.wait("vector", act, sd_act)
            vstat(lambda e: e.reciprocal(stc(RS), stc(SD)))
            vstat(lambda e: e.tensor_scalar_sub(fBs, fBs, stc(M)))
            vstat(lambda e: e.tensor_scalar_mul(fBs, fBs, stc(RS)))

        for dst, src in ((i128, i128_d), (h128, h128_d), (maskt, maskt_d),
                         (ones128, onesb_d), (onesf, onesf_d), (b_sb, bout_d)):
            b.dma("sync", s_cn, lambda e, dst=dst, src=src: e.dma_start(dst[:], src))
        consts_ld = s_cn.n

        J_ARRIVE = [int("{:06b}".format(j)[::-1], 2) for j in range(NJ)]

        def _sim_w2_order():
            lvl = [7] * NJ
            pairs = []
            for s in range(5, -1, -1):
                stp = 1 << s
                for base in range(0, NJ, 2 * stp):
                    for o in range(stp):
                        pairs.append([s, base + o, base + o + stp, False])
            order = []

            def ready(cap=None):
                ct = 0
                progress = True
                while progress:
                    progress = False
                    for pr in pairs:
                        if cap is not None and ct >= cap:
                            return
                        s, a_, b_, done = pr
                        if done or lvl[a_] != s + 1 or lvl[b_] != s + 1:
                            continue
                        if s == 0:
                            order.append(a_)
                            order.append(b_)
                        pr[3] = True
                        lvl[a_] = s
                        lvl[b_] = s
                        ct += 1
                        progress = True

            for ji in range(NJ):
                jds = [J_ARRIVE[ji - 1]] if ji > 0 else []
                if ji == NJ - 1:
                    jds.append(J_ARRIVE[ji])
                for jd in jds:
                    lvl[jd] = 6
                    ready(cap=None if ji == NJ - 1 else 3)
            assert sorted(order) == list(range(NJ))
            return order

        W2_ORDER = _sim_w2_order()

        cast_done = {}
        prep_cast_steps = []
        for nm, src, dst in (("w1", w1_d, w1b_d), ("w3", w3_d, w3b_d)):
            for p in range(16):
                def cf(e, src=src, dst=dst, p=p):
                    return e.dma_start(dst[512 * p:512 * p + 512, :],
                                       src[512 * p:512 * p + 512, :])
                prep_cast_steps.append((cf, nm))
        for p in range(16):
            def cf(e, p=p):
                return e.dma_start(w2b_d[128 * p:128 * p + 128, :],
                                   w2_d[128 * p:128 * p + 128, :])
            prep_cast_steps.append((cf, "w2"))

        bcb = b_bc.bitcast(BF16)
        w13q_steps = []
        for nm, wb_d, wt_d in (("w1", w1b_d, w1t_d), ("w3", w3b_d, w3t_d)):
            for c in range(NC):
                for q in range(4):
                    def trf(e, qs, wb_d=wb_d, c=c, q=q):
                        return e.dma_start(
                            bigB[:, 2048 * qs:2048 * qs + 2048],
                            wb_d[2048 * q:2048 * q + 2048, 128 * c:128 * c + 128],
                            transpose=True)

                    def stf(e, qs, wt_d=wt_d, c=c, q=q):
                        dst = wt_d.rearrange("j p h -> p j h")[
                            :, 16 * q:16 * q + 16, 128 * c:128 * c + 128]
                        src = bigB[:, 2048 * qs:2048 * qs + 2048] \
                            .rearrange("p (j h) -> p j h", j=16)
                        return e.dma_start(dst, src)
                    w13q_steps.append((nm, trf, stf))

        w13_state = {"i": 0}
        w13q_trf_tg = {}

        def _w13_stf(idx):
            qs = idx % 2
            stf = w13q_steps[idx][2]
            b.wait("gpsimd", s_qtr[qs], w13q_trf_tg[idx])
            b.dma("gpsimd", t_qst[qs], lambda e, stf=stf, qs=qs: stf(e, qs))

        def emit_w13q(k):
            for _ in range(k):
                i = w13_state["i"]
                if i >= len(w13q_steps):
                    return
                nm = w13q_steps[i][0]
                if cast_done.get(nm) is None:
                    return
                if i >= 2:
                    _w13_stf(i - 2)
                qs = i % 2
                trf = w13q_steps[i][1]
                b.wait("sync", g_c[nm], cast_done[nm])
                b.wait("sync", t_qst[qs], t_qst[qs].n)
                w13q_trf_tg[i] = b.dma(
                    "sync", s_qtr[qs], lambda e, trf=trf, qs=qs: trf(e, qs))
                w13_state["i"] = i + 1

        w2q_steps = []
        for j in W2_ORDER:
            def trf2(e, qs, j=j):
                return e.dma_start(bcb[:, 2048 * qs:2048 * qs + 2048],
                                   w2b_d[:, 128 * j:128 * j + 128],
                                   transpose=True)

            def stf2(e, qs, j=j):
                return e.dma_start(w2t_d[j], bcb[:, 2048 * qs:2048 * qs + 2048])
            w2q_steps.append((j, trf2, stf2))

        w2_state = {"i": 0, "stf": 0}
        w2q_trf_tg = {}
        w2_ts_tg = {}
        w2_guard = {"pr": NONE}

        def _w2_stf_upto(lim):
            while w2_state["stf"] < min(lim, w2_state["i"]):
                idx = w2_state["stf"]
                qs = idx % 2
                j, _, stf = w2q_steps[idx]
                b.wait("gpsimd", s_q2r[qs], w2q_trf_tg[idx])
                b.dma("gpsimd", t_q2s[qs], lambda e, stf=stf, qs=qs: stf(e, qs))
                w2_ts_tg[j] = (t_q2s[qs], t_q2s[qs].n)
                w2_state["stf"] = idx + 1

        def emit_w2q(k):
            for _ in range(k):
                i = w2_state["i"]
                if i >= len(w2q_steps):
                    return
                if cast_done.get("w2") is None:
                    return
                _w2_stf_upto(i - 1)
                qs = i % 2
                trf = w2q_steps[i][1]
                b.waitp("sync", w2_guard["pr"])
                b.wait("sync", g_c["w2"], cast_done["w2"])
                b.wait("sync", t_q2s[qs], t_q2s[qs].n)
                w2q_trf_tg[i] = b.dma(
                    "sync", s_q2r[qs], lambda e, trf=trf, qs=qs: trf(e, qs))
                w2_state["i"] = i + 1

        prep_state = {"cast_i": 0}

        def emit_prep_casts(k):
            for _ in range(k):
                i = prep_state["cast_i"]
                if i >= len(prep_cast_steps):
                    return
                fn, nm = prep_cast_steps[i]
                v = b.dma("gpsimd", g_c[nm], fn)
                if g_c[nm].n == 256:
                    cast_done[nm] = v
                prep_state["cast_i"] = i + 1

        b.wait("vector", s_cn, consts_ld)
        b.emit("vector", lambda e: e.tensor_copy(
            fB[:, 0:128], i128[:]).then_inc(dve.h, 1))
        idf_dve = dve.inc()
        xbt_free = NONE
        p1_bank_free = [NONE, NONE]
        fa1_free = [NONE, NONE]
        for t in range(NC):
            slot = t % 2
            b.waitp("sync", fa1_free[slot])
            ld = b.dma("sync", s_fa[slot], lambda e, t=t, slot=slot: e.dma_start(
                fA[:, 2048 * slot:2048 * slot + 2048], x_d[128 * t:128 * t + 128, :]))
            if t < 8:
                emit_prep_casts(2)
            b.wait("tensor", s_fa[slot], ld)
            b.wait("tensor", s_cn, consts_ld)
            b.wait("tensor", dve, idf_dve)
            for q4 in range(4):
                tq = q4 % 2
                bank = 1024 + 512 * tq
                b.waitp("tensor", p1_bank_free[tq])
                for kq in range(4):
                    c = 4 * q4 + kq
                    last = (kq == 3)
                    def mmf(e, c=c, bank=bank, kq=kq, last=last, slot=slot):
                        mm = e.matmul(pbB[:, bank + 128 * kq:bank + 128 * kq + 128],
                                      fA[:, 2048 * slot + 128 * c:2048 * slot + 128 * c + 128],
                                      fB[:, 0:128], start=True, stop=True)
                        if last:
                            mm.then_inc(pe.h, 1)
                    b.emit("tensor", mmf)
                q_pe = pe.inc()
                b.wait("scalar", pe, q_pe)
                def evf(e, q4=q4, t=t, bank=bank):
                    dst = bigA[:].rearrange("p (c q) -> p c q", c=16)[
                        :, 4 * q4:4 * q4 + 4, 128 * t:128 * t + 128]
                    e.copy(dst, pbB[:, bank:bank + 512]
                           .rearrange("p (c q) -> p c q", c=4)).then_inc(act.h, 1)
                b.emit("scalar", evf)
                p1_bank_free[tq] = (act, act.inc())
            fa1_free[slot] = (pe, pe.n)
        xT_done_act = act.n

        wv_free = NONE
        btmp_free = [NONE] * 4
        v_bank_free = [NONE] * 4
        for nnc in range(4):
            b.waitp("gpsimd", wv_free)
            def wvf(e, nnc=nnc):
                src = wqkv_d[:, 2 * D + 512 * nnc:2 * D + 512 * nnc + 512]
                return e.dma_start(warena[:].rearrange("p (c n) -> p c n", c=16),
                                   src.rearrange("(c p) n -> p c n", p=128))
            ldv = b.dma("gpsimd", g_wv, wvf)
            emit_prep_casts(2)
            emit_w13q(6)
            b.wait("tensor", g_wv, ldv)
            b.wait("tensor", act, xT_done_act)
            for t in range(NC):
                bk = t % 4
                bank = 512 * bk
                b.waitp("tensor", v_bank_free[bk])
                for c in range(NC):
                    last = (c == NC - 1)
                    def mmf(e, t=t, c=c, bank=bank, last=last):
                        mm = e.matmul(pbA[:, bank:bank + 512],
                                      bigA[:, 2048 * c + 128 * t:2048 * c + 128 * t + 128],
                                      warena[:, 512 * c:512 * c + 512],
                                      start=(c == 0), stop=last)
                        if last:
                            mm.then_inc(pe.h, 1)
                    b.emit("tensor", mmf)
                v_pe = pe.inc()
                b.waitp("vector", btmp_free[bk])
                b.wait("vector", pe, v_pe)
                b.emit("vector", lambda e, bk=bk, bank=bank: e.tensor_copy(
                    btmp[:, 512 * bk:512 * bk + 512],
                    pbA[:, bank:bank + 512]).then_inc(dve.h, 1))
                ev_dve = dve.inc()
                v_bank_free[bk] = (dve, ev_dve)
                b.wait("gpsimd", dve, ev_dve)
                stv = b.dma("gpsimd", t_bt[bk], lambda e, t=t, nnc=nnc, bk=bk: e.dma_start(
                    vsc_d[t][:, 512 * nnc:512 * nnc + 512],
                    btmp[:, 512 * bk:512 * bk + 512]))
                btmp_free[bk] = (t_bt[bk], stv)
            wv_free = (pe, pe.n)
        pbA_free = (dve, dve.n)
        vsc_totals = [(s, s.n) for s in t_bt]

        wqk_free = [wv_free, wv_free]
        vsl_free = [NONE, NONE]
        outt_free = [NONE, NONE]
        pt_free = NONE
        vh_lds = {}
        ldqks = {}
        exp_free = [pbA_free] * 4
        pv_free = [NONE, NONE]
        sum_free = [NONE, NONE]
        qk_free = NONE
        kres_free = NONE

        for h in range(NH):
            wslot = (h // 2) % 2
            arena = warena if wslot == 0 else fAb
            if h % 2 == 0:
                hp = h // 2
                b.waitp("gpsimd", wqk_free[wslot])
                def wqf(e, hp=hp, arena=arena):
                    src = wqkv_d[:, 256 * hp:256 * hp + 256]
                    return e.dma_start(
                        arena[:, 0:4096].rearrange("p (c d) -> p c d", c=16),
                        src.rearrange("(c p) d -> p c d", p=128))
                def wkf(e, hp=hp, arena=arena):
                    src = wqkv_d[:, D + 256 * hp:D + 256 * hp + 256]
                    return e.dma_start(
                        arena[:, 4096:8192].rearrange("p (c d) -> p c d", c=16),
                        src.rearrange("(c p) d -> p c d", p=128))
                b.dma("gpsimd", g_qk[wslot], wqf)
                ldqks[h // 2] = b.dma("gpsimd", g_qk[wslot], wkf)
            ldqk = ldqks[h // 2]

            b.wait("tensor", g_qk[wslot], ldqk)
            for pr in exp_free + pv_free + sum_free + [kres_free]:
                b.waitp("tensor", pr)
            hoff = 128 * (h % 2)
            for dst_ps, woff in ((pbA, hoff), (pbB, 4096 + hoff)):
                for c in range(NC):
                    for nn in range(4):
                        last = (c == NC - 1)
                        def mmf(e, dst_ps=dst_ps, woff=woff, nn=nn, c=c, last=last,
                                arena=arena):
                            mm = e.matmul(dst_ps[:, 512 * nn:512 * nn + 512],
                                          arena[:, woff + 256 * c:woff + 256 * c + 128],
                                          bigA[:, 2048 * c + 512 * nn:2048 * c + 512 * nn + 512],
                                          start=(c == 0), stop=last)
                            if last and nn == 3:
                                mm.then_inc(pe.h, 1)
                        b.emit("tensor", mmf)
                pe.inc()
            qkv_mm_pe = pe.n
            b.waitp("scalar", qk_free)
            b.wait("scalar", pe, qkv_mm_pe)
            b.emit("scalar", lambda e: e.copy(qk[:, 0:2048], pbA[:, 0:2048]).then_inc(act.h, 1))
            act.inc()
            b.emit("scalar", lambda e: e.copy(qk[:, 2048:4096], pbB[:, 0:2048]).then_inc(act.h, 1))
            qk_ready = act.inc()
            if h % 2 == 1:
                wqk_free[wslot] = (pe, qkv_mm_pe)
            exp_free = [(act, qk_ready)] * 4
            pv_free = [(act, qk_ready), (act, qk_ready)]
            sum_free = [(act, qk_ready), (act, qk_ready)]

            vslot = h % 2

            def emit_vhead(hh):
                vs = hh % 2
                for s_, v_ in vsc_totals:
                    b.wait("sync", s_, v_)
                b.waitp("sync", vsl_free[vs])
                def vhf(e, hh=hh, vs=vs):
                    src = vsc_d[:, :, 128 * hh:128 * hh + 128].rearrange("j p d -> p j d")
                    return e.dma_start(vsl[:, 2048 * vs:2048 * vs + 2048]
                                       .rearrange("p (j d) -> p j d", j=16), src)
                vh_lds[hh] = b.dma("sync", s_vs[vs], vhf)

            if h == 0:
                emit_vhead(0)
            if h + 1 < NH:
                emit_vhead(h + 1)
            vh_ld = vh_lds[h]

            if h < 12:
                emit_prep_casts(2)
            emit_w13q(8 if h < 10 else 5)
            if h >= 12:
                emit_w2q(8)

            last_tt_dve = 0
            for qp in range(2):
                qcs = (2 * qp, 2 * qp + 1)
                njv = 4 * qcs[1] + 4
                b.wait("tensor", act, qk_ready)
                b.waitp("tensor", pt_free)
                b.wait("tensor", s_vs[vslot], vh_ld)
                b.waitp("scalar", pt_free)
                exp_acts = {}
                for j in range(njv):
                    live = [m for m in range(2) if j <= 4 * qcs[m] + 3]
                    for m in live:
                        qc = qcs[m]
                        sbank = (2 * j + m) % 4
                        b.waitp("tensor", exp_free[sbank])
                        def smf(e, j=j, qc=qc, sbank=sbank):
                            e.matmul(pbA[:, 512 * sbank:512 * sbank + 512],
                                     qk[:, 2048 + 128 * j:2048 + 128 * j + 128],
                                     qk[:, 512 * qc:512 * qc + 512],
                                     start=True, stop=True).then_inc(pe.h, 1)
                        b.emit("tensor", smf)
                        s_pe = pe.inc()
                        r = j - 4 * qc
                        if 0 <= r <= 3:
                            b.waitp("vector", exp_free[sbank])
                            b.wait("vector", pe, s_pe)
                            b.emit("vector", lambda e, sbank=sbank, r=r: e.tensor_add(
                                pbA[:, 512 * sbank:512 * sbank + 512],
                                pbA[:, 512 * sbank:512 * sbank + 512],
                                maskt[:, 512 * r:512 * r + 512]).then_inc(dve.h, 1))
                            mk = dve.inc()
                            b.wait("scalar", dve, mk)
                        else:
                            b.wait("scalar", pe, s_pe)
                        def expf(e, j=j, m=m, sbank=sbank):
                            e.activation(pt[:, 1024 * j + 512 * m:1024 * j + 512 * m + 512],
                                         pbA[:, 512 * sbank:512 * sbank + 512],
                                         AF.Exp, scale=SCALE).then_inc(act.h, 1)
                        b.emit("scalar", expf)
                        exp_free[sbank] = (act, act.inc())
                        exp_acts[(j, m)] = act.n
                for m in range(2):
                    b.waitp("tensor", pv_free[m])
                    b.waitp("tensor", sum_free[m])
                for j in range(njv):
                    for m in range(2):
                        if j > 4 * qcs[m] + 3:
                            continue
                        b.wait("tensor", act, exp_acts[(j, m)])
                        nj_m = 4 * qcs[m] + 4
                        def pvf(e, j=j, m=m, vslot=vslot, last=(j == 4 * qcs[m] + 3)):
                            mm = e.matmul(pbB[:, 512 * m:512 * m + 512],
                                          vsl[:, 2048 * vslot + 128 * j:2048 * vslot + 128 * j + 128],
                                          pt[:, 1024 * j + 512 * m:1024 * j + 512 * m + 512],
                                          start=(j == 0), stop=last)
                            if last:
                                mm.then_inc(pe.h, 1)
                        b.emit("tensor", pvf)
                        if j == 4 * qcs[m] + 3:
                            pv_pes = pe.inc()
                        def suf(e, j=j, m=m, last=(j == 4 * qcs[m] + 3)):
                            mm = e.matmul(pbB[:, 1024 + 512 * m:1536 + 512 * m],
                                          ones128[:],
                                          pt[:, 1024 * j + 512 * m:1024 * j + 512 * m + 512],
                                          start=(j == 0), stop=last)
                            if last:
                                mm.then_inc(pe.h, 1)
                        b.emit("tensor", suf)
                        if j == 4 * qcs[m] + 3:
                            sum_pes = pe.inc()
                            qc = qcs[m]
                            b.wait("vector", pe, sum_pes)
                            b.emit("vector", lambda e, m=m: e.reciprocal(
                                fB[:, 512 * m:512 * m + 512],
                                pbB[:, 1024 + 512 * m:1536 + 512 * m]).then_inc(dve.h, 1))
                            rc_dve = dve.inc()
                            sum_free[m] = (dve, rc_dve)
                            b.wait("vector", dve, rc_dve)
                            b.wait("vector", pe, pv_pes)
                            oslot = h % 2
                            if qc == 0:
                                b.waitp("vector", outt_free[oslot])
                            b.emit("vector", lambda e, m=m, qc=qc, oslot=oslot: e.tensor_mul(
                                outt[:, 2048 * oslot + 512 * qc:2048 * oslot + 512 * qc + 512],
                                pbB[:, 512 * m:512 * m + 512],
                                fB[:, 512 * m:512 * m + 512]).then_inc(dve.h, 1))
                            last_tt_dve = dve.inc()
                            pv_free[m] = (dve, last_tt_dve)
                pt_free = (pe, pe.n)
            b.wait("sync", dve, last_tt_dve)
            sto_t = b.dma("sync", t_ot[h % 2], lambda e, h=h, oslot=h % 2: e.dma_start(
                att_d[h], outt[:, 2048 * oslot:2048 * oslot + 2048]))
            outt_free[h % 2] = (t_ot[h % 2], sto_t)
            qk_free = (pe, pe.n)
            vsl_free[vslot] = (pe, pe.n)
            kres_free = (dve, last_tt_dve)

        pbA_free = (act, act.n)
        att_done_dve = dve.n
        att_pe_done = pe.n
        att_totals = [(s, s.n) for s in t_ot]
        emit_w13q(999)
        assert w13_state["i"] == len(w13q_steps), "w13 prep not fully emitted"
        _w13_stf(len(w13q_steps) - 2)
        _w13_stf(len(w13q_steps) - 1)
        w13_done = [(t_qst[0], t_qst[0].n), (t_qst[1], t_qst[1].n)]
        _w2_stf_upto(10**9)
        w2_sofar = [(t_q2s[0], t_q2s[0].n), (t_q2s[1], t_q2s[1].n)]
        assert prep_state["cast_i"] == len(prep_cast_steps)

        b.wait("gpsimd", pe, att_pe_done)
        for hh in range(NC):
            b.dma("gpsimd", g_wo, lambda e, hh=hh: e.dma_start(
                bigA[:, 2048 * hh:2048 * hh + 2048],
                wout_d[128 * hh:128 * hh + 128, :]))
        wout_ld = g_wo.n

        b.wait("tensor", s_cn, consts_ld)
        b.waitp("tensor", pbA_free)
        for nn in range(4):
            last = (nn == 3)
            def mmf(e, nn=nn, last=last):
                mm = e.matmul(pbA[:, 512 * nn:512 * nn + 512], onesf[:],
                              b_sb[:, 512 * nn:512 * nn + 512], start=True, stop=True)
                if last:
                    mm.then_inc(pe.h, 1)
            b.emit("tensor", mmf)
        bbc_pe = pe.inc()
        b.wait("vector", pe, bbc_pe)
        for s_, v_ in w2_sofar:
            b.wait("vector", s_, v_)
        b.emit("vector", lambda e: e.tensor_copy(b_bc[:], pbA[:, 0:2048]).then_inc(dve.h, 1))
        bbc_dve = dve.inc()

        fA_free = [wqk_free[1], wqk_free[1]]
        fB_free = [(dve, att_done_dve), NONE]
        proj_ps_free = [(dve, bbc_dve), (act, act.n)]
        proj_lds = {}

        def emit_proj_loads(t):
            tslot = t % 2
            for s_, v_ in att_totals:
                b.wait("sync", s_, v_)
            b.waitp("sync", vsl_free[tslot])
            def asl(e, t=t, tslot=tslot):
                src = att_d[:, :, 128 * t:128 * t + 128].rearrange("h p q -> p h q")
                return e.dma_start(vsl[:, 2048 * tslot:2048 * tslot + 2048]
                                   .rearrange("p (h q) -> p h q", h=16), src)
            asl_ld = b.dma("sync", s_vs[tslot], asl)
            b.waitp("sync", fA_free[tslot])
            xres_ld = b.dma("sync", s_fa[tslot], lambda e, t=t, tslot=tslot: e.dma_start(
                fA[:, 2048 * tslot:2048 * tslot + 2048],
                x_d[128 * t:128 * t + 128, :]))
            proj_lds[t] = (asl_ld, xres_ld)

        emit_proj_loads(0)
        for t in range(NC):
            tslot = t % 2
            PS = pbA if tslot == 0 else pbB
            if t + 1 < NC:
                emit_proj_loads(t + 1)
            asl_ld, xres_ld = proj_lds[t]

            b.wait("tensor", g_wo, wout_ld)
            b.wait("tensor", s_vs[tslot], asl_ld)
            b.waitp("tensor", proj_ps_free[tslot])
            for hh in range(NC):
                for nn in range(4):
                    last = (hh == NC - 1)
                    def mmf(e, PS=PS, tslot=tslot, hh=hh, nn=nn, last=last):
                        mm = e.matmul(PS[:, 512 * nn:512 * nn + 512],
                                      vsl[:, 2048 * tslot + 128 * hh:2048 * tslot + 128 * hh + 128],
                                      bigA[:, 2048 * hh + 512 * nn:2048 * hh + 512 * nn + 512],
                                      start=(hh == 0), stop=last)
                        if last and nn == 3:
                            mm.then_inc(pe.h, 1)
                    b.emit("tensor", mmf)
            proj_pe = pe.inc()
            vsl_free[tslot] = (pe, proj_pe)
            b.wait("vector", pe, proj_pe)
            b.wait("vector", s_fa[tslot], xres_ld)
            b.wait("vector", dve, bbc_dve)
            b.waitp("vector", fB_free[0])
            b.emit("vector", lambda e, PS=PS: e.tensor_add(
                fB[:, 0:2048], PS[:, 0:2048], b_bc[:]).then_inc(dve.h, 1))
            proj_ps_free[tslot] = (dve, dve.inc())
            b.emit("vector", lambda e, tslot=tslot: e.tensor_add(
                fB[:, 0:2048], fB[:, 0:2048],
                fA[:, 2048 * tslot:2048 * tslot + 2048]).then_inc(dve.h, 1))
            dve.inc()
            fA_free[tslot] = (dve, dve.n)
            if debug:
                b.wait("sync", dve, dve.n)
                b.dma("sync", t_dbg, lambda e, t=t: e.dma_start(dbg_y1_d[t], fB[:, 0:2048]))
                b.wait("vector", t_dbg, t_dbg.n)
            emit_ln()
            b.wait("sync", dve, dve.n)
            b.dma("sync", t_fb, lambda e, t=t: e.dma_start(x1sc_d[t], fB[:, 0:2048]))
            fB_free[0] = (t_fb, t_fb.n)

        w2_guard["pr"] = (dve, dve.n)

        x1_store_done = t_fb.n
        hid_free = (pe, pe.n)
        barrier = (dve, dve.n)
        w13_free = [NONE, NONE]
        w2_free = [NONE] * 8
        sg_free = [NONE, NONE]
        gu_bank_free = [barrier] * 4
        sa_bank_free = [barrier, barrier]
        w2_ps_free = [barrier, barrier]
        x1t_free = NONE
        first_chunk = True

        def emit_x1t_build(kb):
            nonlocal x1t_free, first_chunk, xbt_free
            for tl in range(4):
                t = 4 * kb + tl
                aslot = tl % 2
                b.wait("sync", t_fb, x1_store_done)
                b.waitp("sync", fA_free[aslot])
                x1_ld = b.dma("sync", s_fa[aslot], lambda e, t=t, aslot=aslot: e.dma_start(
                    fA[:, 2048 * aslot:2048 * aslot + 2048], x1sc_d[t]))
                b.waitp("vector", xbt_free)
                b.wait("vector", s_fa[aslot], x1_ld)
                b.emit("vector", lambda e, aslot=aslot: e.tensor_copy(
                    xbt[:, 0:2048],
                    fA[:, 2048 * aslot:2048 * aslot + 2048]).then_inc(dve.h, 1))
                cv_dve = dve.inc()
                fA_free[aslot] = (dve, cv_dve)
                b.wait("tensor", dve, cv_dve)
                b.waitp("tensor", x1t_free)
                for q4 in range(4):
                    tq = q4 % 2
                    b.waitp("tensor", sa_bank_free[tq])
                    for kq in range(4):
                        c = 4 * q4 + kq
                        last = (kq == 3)
                        def mmf(e, c=c, tq=tq, kq=kq, last=last):
                            mm = e.matmul(pbB[:, 512 * tq + 128 * kq:512 * tq + 128 * kq + 128],
                                          xbt[:, 128 * c:128 * c + 128],
                                          i128[:], start=True, stop=True)
                            if last:
                                mm.then_inc(pe.h, 1)
                        b.emit("tensor", mmf)
                    tq_pe = pe.inc()
                    if first_chunk:
                        for s_, v_ in w13_done:
                            b.wait("scalar", s_, v_)
                        first_chunk = False
                    b.wait("scalar", pe, tq_pe)
                    def evf(e, q4=q4, tq=tq, tl=tl):
                        dst = bigB[:].rearrange("p (c x) -> p c x", c=16)[
                            :, 4 * q4:4 * q4 + 4, 128 * tl:128 * tl + 128]
                        e.copy(dst, pbB[:, 512 * tq:512 * tq + 512]
                               .rearrange("p (c x) -> p c x", c=4)).then_inc(act.h, 1)
                    b.emit("scalar", evf)
                    sa_bank_free[tq] = (act, act.inc())
                xbt_free = (pe, pe.n)

        emit_x1t_build(0)
        x1t_act = act.n
        bf_state = {"ct": 0, "slot": [0] * 4, "tile": [0] * NJ, "fin": []}

        w2_lds = {}
        NW2 = 8 * NJ

        def emit_w2_load(g):
            if g >= NW2:
                return
            slot = g % 8
            jt = W2_ORDER[g % NJ]
            q = "sync" if slot % 2 == 0 else "gpsimd"
            b.waitp(q, w2_ts_tg[jt])
            b.waitp(q, w2_free[slot])
            w2_lds[g] = b.dma(q, s_w2[slot], lambda e, j=jt, slot=slot:
                              e.dma_start(pt[:, 2048 * slot:2048 * slot + 2048],
                                          w2t_d[j]))

        w2_free = [(pe, pe.n)] * 8
        for gp in range(8):
            emit_w2_load(gp)

        for kk in range(4):
            bf_state["fin"] = []
            lvl = [7] * NJ
            pairs = []
            for s in range(5, -1, -1):
                stp = 1 << s
                for base in range(0, NJ, 2 * stp):
                    for o in range(stp):
                        pairs.append([s, base + o, base + o + stp, False])

            def emit_ready_pairs(cap=None):
                ct = 0
                progress = True
                while progress:
                    progress = False
                    for pr in pairs:
                        if cap is not None and ct >= cap:
                            return
                        s, a_, b_, done = pr
                        if done or lvl[a_] != s + 1 or lvl[b_] != s + 1:
                            continue
                        a0, b0 = 512 * a_, 512 * b_
                        fin = (s == 0)
                        bs = bf_state["ct"] % 4
                        bf_state["ct"] += 1
                        b.wait("vector", act, bf_state["slot"][bs])
                        b.wait("vector", act, max(bf_state["tile"][a_],
                                                  bf_state["tile"][b_]))
                        b.emit("vector", lambda e, a0=a0, b0=b0, bs=bs: e.tensor_sub(
                            btmp[:, 512 * bs:512 * bs + 512],
                            bigA[:, a0:a0 + 512],
                            bigA[:, b0:b0 + 512]).then_inc(dve.h, 1))
                        sub_dve = dve.inc()
                        def addf(e, a0=a0, b0=b0, fin=fin):
                            mm = e.tensor_add(bigA[:, a0:a0 + 512],
                                              bigA[:, a0:a0 + 512], bigA[:, b0:b0 + 512])
                            if fin:
                                mm.then_inc(dve.h, 1)
                        b.emit("vector", addf)
                        add_tg = dve.inc() if fin else None
                        b.wait("scalar", dve, sub_dve)
                        b.emit("scalar", lambda e, b0=b0, bs=bs: e.copy(
                            bigA[:, b0:b0 + 512],
                            btmp[:, 512 * bs:512 * bs + 512]).then_inc(act.h, 1))
                        cp_act = act.inc()
                        bf_state["slot"][bs] = cp_act
                        bf_state["tile"][b_] = cp_act
                        if fin:
                            bf_state["fin"].append((a_, add_tg, b_, cp_act))
                        pr[3] = True
                        lvl[a_] = s
                        lvl[b_] = s
                        ct += 1
                        progress = True

            mult_dves = {}
            for ji in range(NJ):
                if kk == 0:
                    emit_w2q(1)
                j = J_ARRIVE[ji]
                jslot = ji % 2
                for s_, v_ in w13_done:
                    b.wait("sync", s_, v_)
                b.waitp("sync", w13_free[jslot])
                b.dma("sync", s_w13[jslot], lambda e, j=j, jslot=jslot: e.dma_start(
                    warena[:, 4096 * jslot:4096 * jslot + 2048], w1t_d[j]))
                for s_, v_ in w13_done:
                    b.wait("gpsimd", s_, v_)
                b.waitp("gpsimd", w13_free[jslot])
                slab_ld = b.dma("gpsimd", s_w13[jslot], lambda e, j=j, jslot=jslot: e.dma_start(
                    warena[:, 4096 * jslot + 2048:4096 * jslot + 4096], w3t_d[j]))

                b.wait("tensor", s_w13[jslot], slab_ld)
                b.wait("tensor", act, x1t_act)
                for woff, bank in ((4096 * jslot, 512 * jslot),
                                   (4096 * jslot + 2048, 1024 + 512 * jslot)):
                    b.waitp("tensor", gu_bank_free[bank // 512])
                    for c in range(NC):
                        last = (c == NC - 1)
                        def mmf(e, woff=woff, bank=bank, c=c, last=last):
                            mm = e.matmul(pbA[:, bank:bank + 512],
                                          warena[:, woff + 128 * c:woff + 128 * c + 128],
                                          bigB[:, 512 * c:512 * c + 512],
                                          start=(c == 0), stop=last)
                            if last:
                                mm.then_inc(pe.h, 1)
                        b.emit("tensor", mmf)
                    pe.inc()
                gu_pe = pe.n
                w13_free[jslot] = (pe, gu_pe)
                sslot = ji % 2
                b.waitp("scalar", sg_free[sslot])
                b.wait("scalar", pe, gu_pe)
                b.emit("scalar", lambda e, sslot=sslot, jslot=jslot: e.activation(
                    sg[:, 512 * sslot:512 * sslot + 512],
                    pbA[:, 512 * jslot:512 * jslot + 512], AF.Silu).then_inc(act.h, 1))
                silu_act = act.inc()
                gu_bank_free[jslot] = (act, silu_act)
                if j == 0:
                    b.waitp("vector", hid_free)
                b.wait("vector", act, silu_act)
                b.emit("vector", lambda e, j=j, sslot=sslot, jslot=jslot: e.tensor_mul(
                    bigA[:, 512 * j:512 * j + 512],
                    pbA[:, 1024 + 512 * jslot:1024 + 512 * jslot + 512],
                    sg[:, 512 * sslot:512 * sslot + 512]).then_inc(dve.h, 1))
                mult_dve = dve.inc()
                gu_bank_free[2 + jslot] = (dve, mult_dve)
                sg_free[sslot] = (dve, mult_dve)
                mult_dves[j] = mult_dve
                jds_l = [J_ARRIVE[ji - 1]] if ji > 0 else []
                if ji == NJ - 1:
                    jds_l.append(J_ARRIVE[ji])
                for pos, jd in enumerate(jds_l):
                    jds = (ji - 1 + pos) % 2
                    b.wait("tensor", dve, mult_dves[jd])
                    b.waitp("tensor", sa_bank_free[jds])
                    def mmf(e, jd=jd, jds=jds):
                        e.matmul(pbB[:, 512 * jds:512 * jds + 512], h128[:],
                                 bigA[:, 512 * jd:512 * jd + 512],
                                 start=True, stop=True).then_inc(pe.h, 1)
                    b.emit("tensor", mmf)
                    sa_pe = pe.inc()
                    b.wait("vector", pe, sa_pe)
                    b.emit("vector", lambda e, jd=jd, jds=jds: e.tensor_scalar_mul(
                        bigA[:, 512 * jd:512 * jd + 512],
                        pbB[:, 512 * jds:512 * jds + 512], WHT_SCALE).then_inc(dve.h, 1))
                    sa_bank_free[jds] = (dve, dve.inc())
                    lvl[jd] = 6
                    emit_ready_pairs(cap=None if ji == NJ - 1 else 3)
            x1t_free = (pe, pe.n)
            assert all(pr[3] for pr in pairs)
            if kk == 0:
                emit_w2q(999)
                assert w2_state["i"] == len(w2q_steps)
                _w2_stf_upto(10**9)
                w2_all = [(t_q2s[0], t_q2s[0].n), (t_q2s[1], t_q2s[1].n)]
            order = []
            tile_wait = {}
            for a_, dtg, b_, atg in bf_state["fin"]:
                order.append(a_)
                tile_wait[a_] = (dve, dtg)
                order.append(b_)
                tile_wait[b_] = (act, atg)
            assert order == W2_ORDER

            hid_ready = dve.n
            hid_act = act.n
            if debug and kk == 1:
                b.wait("sync", dve, hid_ready)
                b.wait("sync", act, hid_act)
                b.dma("sync", t_dbg, lambda e: e.dma_start(dbg_hid_d, bigA[:]))
                b.wait("tensor", t_dbg, t_dbg.n)

            for tc2 in range(2):
                b.waitp("tensor", w2_ps_free[0])
                b.waitp("tensor", w2_ps_free[1])
                gbase = (kk * 2 + tc2) * NJ
                x1r_lds = {}

                def emit_x1r_loads():
                    for t2 in range(2):
                        tg = 4 * kk + 2 * tc2 + t2
                        b.wait("sync", t_fb, x1_store_done)
                        b.waitp("sync", fA_free[t2])
                        x1r_lds[t2] = b.dma("sync", s_fa[t2], lambda e, tg=tg, aslot=t2:
                                            e.dma_start(fA[:, 2048 * aslot:2048 * aslot + 2048],
                                                        x1sc_d[tg]))

                for idx in range(NJ):
                    if idx == 40:
                        emit_x1r_loads()
                    tile = W2_ORDER[idx]
                    wsem, wtg = tile_wait[tile]
                    b.wait("tensor", wsem, wtg)
                    j4 = (gbase + idx) % 8
                    b.wait("tensor", s_w2[j4], w2_lds[gbase + idx])
                    for t2 in range(2):
                        PS = pbA if t2 == 0 else pbB
                        tloc = 2 * tc2 + t2
                        for nn in range(4):
                            last_j = (idx == NJ - 1)
                            last_mm = (t2 == 1 and nn == 3)
                            def mmf(e, PS=PS, tile=tile, j4=j4, tloc=tloc, nn=nn,
                                    idx=idx, last_j=last_j, last_mm=last_mm):
                                mm = e.matmul(PS[:, 512 * nn:512 * nn + 512],
                                              bigA[:, 512 * tile + 128 * tloc:512 * tile + 128 * tloc + 128],
                                              pt[:, 2048 * j4 + 512 * nn:2048 * j4 + 512 * nn + 512],
                                              start=(idx == 0), stop=last_j)
                                if last_mm:
                                    mm.then_inc(pe.h, 1)
                            b.emit("tensor", mmf)
                    w2_free[j4] = (pe, pe.inc())
                    emit_w2_load(gbase + idx + 8)
                w2_pe = pe.n
                for t2 in range(2):
                    PS = pbA if t2 == 0 else pbB
                    dst = fB[:, 0:2048] if t2 == 0 else b_bc[:]
                    if kk == 0 and t2 == 1:
                        for s_, v_ in w2_all:
                            b.wait("vector", s_, v_)
                    b.wait("vector", pe, w2_pe)
                    b.wait("vector", s_fa[t2], x1r_lds[t2])
                    b.waitp("vector", fB_free[t2])
                    b.emit("vector", lambda e, PS=PS, dst=dst, t2=t2: e.tensor_add(
                        dst, PS[:, 0:2048],
                        fA[:, 2048 * t2:2048 * t2 + 2048]).then_inc(dve.h, 1))
                    w2_ps_free[t2] = (dve, dve.inc())
                    fA_free[t2] = (dve, dve.n)
                if tc2 == 1 and kk < 3:
                    sa_bank_free = [w2_ps_free[1], w2_ps_free[1]]
                    emit_x1t_build(kk + 1)
                    x1t_act = act.n
                for t2 in range(2):
                    tg = 4 * kk + 2 * tc2 + t2
                    dst = fB[:, 0:2048] if t2 == 0 else b_bc[:]
                    emit_ln(None if t2 == 0 else b_bc[:])
                    b.wait("sync", dve, dve.n)
                    b.dma("sync", t_fb, lambda e, tg=tg, dst=dst: e.dma_start(
                        y_d[128 * tg:128 * tg + 128, :], dst))
                    fB_free[t2] = (t_fb, t_fb.n)
            hid_free = (pe, pe.n)
            gu_bank_free = [w2_ps_free[0]] * 2 + [w2_ps_free[1]] * 2
            sa_bank_free = [w2_ps_free[1], w2_ps_free[1]]
            w13_free = [(pe, pe.n), (pe, pe.n)]

        b.emit("sync", lambda e, v=t_fb.n: e.wait_ge(t_fb.h, v))

        with nc.Block() as block:
            @block.sync
            def _(e):
                for fn in b.ops["sync"]:
                    fn(e)

            @block.gpsimd
            def _(e):
                for fn in b.ops["gpsimd"]:
                    fn(e)

            @block.tensor
            def _(e):
                for fn in b.ops["tensor"]:
                    fn(e)

            @block.scalar
            def _(e):
                for fn in b.ops["scalar"]:
                    fn(e)

            @block.vector
            def _(e):
                for fn in b.ops["vector"]:
                    fn(e)

    counts = {e: len(b.ops[e]) for e in ENGINES}
    return nc, counts


_CACHE = {}


def _get_program():
    if "nc" not in _CACHE:
        nc, counts = build()
        _CACHE["nc"] = nc
        _CACHE["counts"] = counts
    return _CACHE["nc"], _CACHE["counts"]


def _consts():
    i128 = np.eye(128, dtype=BF)
    idx = np.arange(128)
    pop = np.zeros((128, 128), dtype=np.int64)
    for bit in range(7):
        pop += ((idx[:, None] >> bit) & 1) * ((idx[None, :] >> bit) & 1)
    h128 = np.where(pop % 2 == 0, 1.0, -1.0).astype(BF)
    p = idx[:, None]
    f = np.arange(512)[None, :]
    regs = [np.where(128 * r + p > f, -1e30, 0.0) for r in range(4)]
    maskt = np.concatenate(regs, axis=1).astype(BF)
    onesb = np.ones((128, 128), dtype=BF)
    onesf = np.ones((1, 128), dtype=np.float32)
    return i128, h128, maskt, onesb, onesf


def kernel(L0_attn_x, L0_attn_w_qkv, L0_attn_w_out, L0_attn_b_out,
           L0_ffn_W1, L0_ffn_W2, L0_ffn_W3):
    nc, _ = _get_program()
    i128, h128, maskt, onesb, onesf = _consts()
    x = np.ascontiguousarray(np.asarray(L0_attn_x, dtype=np.float32))
    wqkv = np.ascontiguousarray(np.asarray(L0_attn_w_qkv, dtype=np.float32))
    wout = np.ascontiguousarray(np.asarray(L0_attn_w_out, dtype=np.float32))
    bout = np.ascontiguousarray(np.asarray(L0_attn_b_out, dtype=np.float32)).reshape(1, D)
    W1 = np.ascontiguousarray(np.asarray(L0_ffn_W1, dtype=np.float32))
    W2 = np.ascontiguousarray(np.asarray(L0_ffn_W2, dtype=np.float32))
    W3 = np.ascontiguousarray(np.asarray(L0_ffn_W3, dtype=np.float32))
    in_maps = []
    for c in range(8):
        in_maps.append({
            "x": np.ascontiguousarray(x[c]),
            "w_qkv": wqkv, "w_out": wout, "b_out": bout,
            "W1": W1, "W2": W2, "W3": W3,
            "i128": i128, "h128": h128, "maskt": maskt, "onesb": onesb,
            "onesf": onesf,
        })
    res = run_bass_kernel_spmd(nc, in_maps, core_ids=list(range(8)))
    out = np.stack([np.asarray(r["y"], dtype=np.float32) for r in res.results], axis=0)
    return out


if __name__ == "__main__":
    nc, counts = _get_program()
    print("instruction list sizes:", counts, "total:", sum(counts.values()))


# version 32
# speedup vs baseline: 1.0661x; 1.0661x over previous
import math
import numpy as np
import ml_dtypes
from contextlib import ExitStack

import concourse.bass as bass
import concourse.mybir as mybir
from concourse.bass_utils import run_bass_kernel_spmd

F32 = mybir.dt.float32
BF16 = mybir.dt.bfloat16
AF = mybir.ActivationFunctionType
ALU = mybir.AluOpType
AX = mybir.AxisListType
BF = ml_dtypes.bfloat16

T = 2048
D = 2048
NH = 16
DH = 128
HF = 8192
NC = 16
NJ = 64
SCALE = 1.0 / math.sqrt(DH)
WHT_SCALE = 1.0 / math.sqrt(HF)
EPS = 1e-5

ENGINES = ("sync", "gpsimd", "tensor", "scalar", "vector")


class Sem:
    def __init__(self, stack, nc, name):
        self.h = stack.enter_context(nc.semaphore(name))
        self.name = name
        self.n = 0

    def inc(self, k=1):
        self.n += k
        return self.n


class B:

    def __init__(self):
        self.ops = {e: [] for e in ENGINES}
        self.last_wait = {}

    def emit(self, eng, fn):
        self.ops[eng].append(fn)

    def wait(self, eng, sem, val):
        if sem is None or val <= 0:
            return
        key = (eng, sem.name)
        if self.last_wait.get(key, 0) >= val:
            return
        self.last_wait[key] = val
        self.emit(eng, lambda e, s=sem.h, v=val: e.wait_ge(s, v))

    def waitp(self, eng, pair):
        self.wait(eng, pair[0], pair[1])

    def dma(self, eng, sem, fn):
        self.emit(eng, lambda e, fn=fn, s=sem.h: fn(e).then_inc(s, 16))
        return sem.inc(16)


def build(debug=False):
    nc = bass.Bass("TRN2", target_bir_lowering=False)
    SCR = "ExternalOutput" if debug else "Internal"

    x_d = nc.dram_tensor("x", [T, D], F32, kind="ExternalInput").ap()
    wqkv_d = nc.dram_tensor("w_qkv", [D, 3 * D], F32, kind="ExternalInput").ap()
    wout_d = nc.dram_tensor("w_out", [D, D], F32, kind="ExternalInput").ap()
    bout_d = nc.dram_tensor("b_out", [1, D], F32, kind="ExternalInput").ap()
    w1_d = nc.dram_tensor("W1", [HF, D], F32, kind="ExternalInput").ap()
    w2_d = nc.dram_tensor("W2", [D, HF], F32, kind="ExternalInput").ap()
    w3_d = nc.dram_tensor("W3", [HF, D], F32, kind="ExternalInput").ap()
    i128_d = nc.dram_tensor("i128", [128, 128], BF16, kind="ExternalInput").ap()
    h128_d = nc.dram_tensor("h128", [128, 128], BF16, kind="ExternalInput").ap()
    maskt_d = nc.dram_tensor("maskt", [128, 2048], BF16, kind="ExternalInput").ap()
    onesb_d = nc.dram_tensor("onesb", [128, 128], BF16, kind="ExternalInput").ap()
    onesf_d = nc.dram_tensor("onesf", [1, 128], F32, kind="ExternalInput").ap()
    y_d = nc.dram_tensor("y", [T, D], F32, kind="ExternalOutput").ap()
    w1b_d = nc.dram_tensor("w1b", [HF, D], BF16, kind="Internal").ap()
    w3b_d = nc.dram_tensor("w3b", [HF, D], BF16, kind="Internal").ap()
    w2b_d = nc.dram_tensor("w2b", [D, HF], BF16, kind="Internal").ap()
    w1t_d = nc.dram_tensor("w1t", [NJ, 128, D], BF16, kind="Internal").ap()
    w3t_d = nc.dram_tensor("w3t", [NJ, 128, D], BF16, kind="Internal").ap()
    w2t_d = nc.dram_tensor("w2t", [NJ, 128, D], BF16, kind="Internal").ap()
    vsc_d = nc.dram_tensor("vsc", [NC, 128, D], BF16, kind=SCR).ap()
    att_d = nc.dram_tensor("att", [NH, 128, T], BF16, kind=SCR).ap()
    x1sc_d = nc.dram_tensor("x1sc", [NC, 128, D], F32, kind=SCR).ap()
    dbg_y1_d = dbg_hid_d = None
    if debug:
        dbg_y1_d = nc.dram_tensor("dbg_y1", [NC, 128, D], F32, kind="ExternalOutput").ap()
        dbg_hid_d = nc.dram_tensor("dbg_hid", [128, NJ * 512], BF16, kind="ExternalOutput").ap()

    with ExitStack() as stack:
        def sb(name, cols, dtype=BF16, parts=128):
            return stack.enter_context(nc.sbuf_tensor(name, [parts, cols], dtype))

        bigA = sb("bigA", 16 * 2048)
        bigB = sb("bigB", 8192)
        qk = sb("qk", 4096)
        pt = sb("pt", 16 * 1024)
        outt = sb("outt", 2 * 2048)
        vsl = sb("vsl", 2 * 2048)
        warena = sb("warena", 8192)
        btmp = sb("btmp", 2048)
        sg = sb("sg", 2 * 512)
        xbt = sb("xbt", 2048)
        i128 = sb("i128s", 128)
        h128 = sb("h128s", 128)
        maskt = sb("maskts", 2048)
        ones128 = sb("ones128s", 128)
        onesf = sb("onesfs", 128, F32, parts=1)
        b_sb = sb("b_sb", 2048, F32, parts=1)
        b_bc = sb("b_bc", 2048, F32)
        fA = sb("fA", 2 * 2048, F32)
        fAb = fA.bitcast(BF16)
        fB = sb("fB", 2048, F32)
        st = sb("st", 16, F32)

        pbA = stack.enter_context(nc.psum_tensor("pbA", [128, 2048], F32))
        pbB = stack.enter_context(nc.psum_tensor("pbB", [128, 2048], F32))

        S = lambda name: Sem(stack, nc, name)
        s_cn = S("s_cn")
        g_xb = [S("g_xb0"), S("g_xb1")]
        g_c = {"w1": S("g_c1"), "w3": S("g_c3"), "w2": S("g_c2")}
        g_wv = S("g_wv")
        g_qk = [S("g_qk0"), S("g_qk1")]
        g_wo = S("g_wo")
        s_tr = S("s_tr")
        t_ps = S("t_ps")
        s_qtr = [S("s_qtr0"), S("s_qtr1")]
        t_qst = [S("t_qst0"), S("t_qst1")]
        s_q2r = [S("s_q2r0"), S("s_q2r1")]
        t_q2s = [S("t_q2s0"), S("t_q2s1")]
        t_bt = [S("t_bt0"), S("t_bt1"), S("t_bt2"), S("t_bt3")]
        s_vs = [S("s_vs0"), S("s_vs1")]
        t_ot = [S("t_ot0"), S("t_ot1")]
        s_fa = [S("s_fa0"), S("s_fa1")]
        t_fb = S("t_fb")
        s_w13 = [S("s_w13a"), S("s_w13b")]
        s_w2 = [S("s_w2%d" % i) for i in range(8)]
        t_dbg = S("t_dbg")
        pe = S("pe")
        act = S("act")
        dve = S("dve")

        b = B()
        NONE = (None, 0)

        MX, BIAS, SUM, RC, S1, M, NM, QS, M2, VAR, SD, RS = range(12)

        def stc(i):
            return st[:, i:i + 1]

        def emit_ln(buf=None):
            fBs = fB[:, 0:2048] if buf is None else buf
            def vstat(fn):
                b.emit("vector", lambda e, fn=fn: fn(e).then_inc(dve.h, 1))
                b.wait("vector", dve, dve.inc())
            vstat(lambda e: e.reduce_sum(stc(S1), fBs, axis=AX.X))
            vstat(lambda e: e.tensor_scalar_mul(stc(M), stc(S1), 1.0 / 2048.0))
            vstat(lambda e: e.tensor_mul(btmp[:, 0:2048], fBs, fBs))
            vstat(lambda e: e.reduce_sum(stc(QS), btmp[:, 0:2048], axis=AX.X))
            vstat(lambda e: e.tensor_mul(stc(M2), stc(M), stc(M)))
            vstat(lambda e: e.tensor_scalar_mul(stc(QS), stc(QS), 1.0 / 2048.0))
            vstat(lambda e: e.tensor_sub(stc(VAR), stc(QS), stc(M2)))
            vstat(lambda e: e.tensor_scalar_add(stc(VAR), stc(VAR), EPS))
            b.wait("scalar", dve, dve.n)
            b.emit("scalar", lambda e: e.sqrt(stc(SD), stc(VAR)).then_inc(act.h, 1))
            sd_act = act.inc()
            b.wait("vector", act, sd_act)
            vstat(lambda e: e.reciprocal(stc(RS), stc(SD)))
            vstat(lambda e: e.tensor_scalar_sub(fBs, fBs, stc(M)))
            vstat(lambda e: e.tensor_scalar_mul(fBs, fBs, stc(RS)))

        for dst, src in ((i128, i128_d), (h128, h128_d), (maskt, maskt_d),
                         (ones128, onesb_d), (onesf, onesf_d), (b_sb, bout_d)):
            b.dma("sync", s_cn, lambda e, dst=dst, src=src: e.dma_start(dst[:], src))
        consts_ld = s_cn.n

        J_ARRIVE = [int("{:06b}".format(j)[::-1], 2) for j in range(NJ)]

        def _sim_w2_order():
            lvl = [7] * NJ
            pairs = []
            for s in range(5, -1, -1):
                stp = 1 << s
                for base in range(0, NJ, 2 * stp):
                    for o in range(stp):
                        pairs.append([s, base + o, base + o + stp, False])
            order = []

            def ready(cap=None):
                ct = 0
                progress = True
                while progress:
                    progress = False
                    for pr in pairs:
                        if cap is not None and ct >= cap:
                            return
                        s, a_, b_, done = pr
                        if done or lvl[a_] != s + 1 or lvl[b_] != s + 1:
                            continue
                        if s == 0:
                            order.append(a_)
                            order.append(b_)
                        pr[3] = True
                        lvl[a_] = s
                        lvl[b_] = s
                        ct += 1
                        progress = True

            for ji in range(NJ):
                jds = [J_ARRIVE[ji - 1]] if ji > 0 else []
                if ji == NJ - 1:
                    jds.append(J_ARRIVE[ji])
                for jd in jds:
                    lvl[jd] = 6
                    ready(cap=None if ji == NJ - 1 else 3)
            assert sorted(order) == list(range(NJ))
            return order

        W2_ORDER = _sim_w2_order()

        cast_done = {}
        prep_cast_steps = []
        for nm, src, dst in (("w1", w1_d, w1b_d), ("w3", w3_d, w3b_d)):
            for p in range(16):
                def cf(e, src=src, dst=dst, p=p):
                    return e.dma_start(dst[512 * p:512 * p + 512, :],
                                       src[512 * p:512 * p + 512, :])
                prep_cast_steps.append((cf, nm))
        for p in range(16):
            def cf(e, p=p):
                return e.dma_start(w2b_d[128 * p:128 * p + 128, :],
                                   w2_d[128 * p:128 * p + 128, :])
            prep_cast_steps.append((cf, "w2"))

        bcb = b_bc.bitcast(BF16)
        w13q_steps = []
        for nm, wb_d, wt_d in (("w1", w1b_d, w1t_d), ("w3", w3b_d, w3t_d)):
            for c in range(NC):
                for q in range(4):
                    def trf(e, qs, wb_d=wb_d, c=c, q=q):
                        return e.dma_start(
                            bigB[:, 2048 * qs:2048 * qs + 2048],
                            wb_d[2048 * q:2048 * q + 2048, 128 * c:128 * c + 128],
                            transpose=True)

                    def stf(e, qs, wt_d=wt_d, c=c, q=q):
                        dst = wt_d.rearrange("j p h -> p j h")[
                            :, 16 * q:16 * q + 16, 128 * c:128 * c + 128]
                        src = bigB[:, 2048 * qs:2048 * qs + 2048] \
                            .rearrange("p (j h) -> p j h", j=16)
                        return e.dma_start(dst, src)
                    w13q_steps.append((nm, trf, stf))

        w13_state = {"i": 0}
        w13q_trf_tg = {}

        def _w13_stf(idx):
            qs = idx % 2
            stf = w13q_steps[idx][2]
            b.wait("gpsimd", s_qtr[qs], w13q_trf_tg[idx])
            b.dma("gpsimd", t_qst[qs], lambda e, stf=stf, qs=qs: stf(e, qs))

        def emit_w13q(k):
            for _ in range(k):
                i = w13_state["i"]
                if i >= len(w13q_steps):
                    return
                nm = w13q_steps[i][0]
                if cast_done.get(nm) is None:
                    return
                if i >= 2:
                    _w13_stf(i - 2)
                qs = i % 2
                trf = w13q_steps[i][1]
                b.wait("sync", g_c[nm], cast_done[nm])
                b.wait("sync", t_qst[qs], t_qst[qs].n)
                w13q_trf_tg[i] = b.dma(
                    "sync", s_qtr[qs], lambda e, trf=trf, qs=qs: trf(e, qs))
                w13_state["i"] = i + 1

        w2q_steps = []
        for j in W2_ORDER:
            def trf2(e, qs, j=j):
                return e.dma_start(bcb[:, 2048 * qs:2048 * qs + 2048],
                                   w2b_d[:, 128 * j:128 * j + 128],
                                   transpose=True)

            def stf2(e, qs, j=j):
                return e.dma_start(w2t_d[j], bcb[:, 2048 * qs:2048 * qs + 2048])
            w2q_steps.append((j, trf2, stf2))

        w2_state = {"i": 0, "stf": 0}
        w2q_trf_tg = {}
        w2_ts_tg = {}
        w2_guard = {"pr": NONE}

        def _w2_stf_upto(lim):
            while w2_state["stf"] < min(lim, w2_state["i"]):
                idx = w2_state["stf"]
                qs = idx % 2
                j, _, stf = w2q_steps[idx]
                b.wait("gpsimd", s_q2r[qs], w2q_trf_tg[idx])
                b.dma("gpsimd", t_q2s[qs], lambda e, stf=stf, qs=qs: stf(e, qs))
                w2_ts_tg[j] = (t_q2s[qs], t_q2s[qs].n)
                w2_state["stf"] = idx + 1

        def emit_w2q(k):
            for _ in range(k):
                i = w2_state["i"]
                if i >= len(w2q_steps):
                    return
                if cast_done.get("w2") is None:
                    return
                _w2_stf_upto(i - 1)
                qs = i % 2
                trf = w2q_steps[i][1]
                b.waitp("sync", w2_guard["pr"])
                b.wait("sync", g_c["w2"], cast_done["w2"])
                b.wait("sync", t_q2s[qs], t_q2s[qs].n)
                w2q_trf_tg[i] = b.dma(
                    "sync", s_q2r[qs], lambda e, trf=trf, qs=qs: trf(e, qs))
                w2_state["i"] = i + 1

        prep_state = {"cast_i": 0}

        def emit_prep_casts(k):
            for _ in range(k):
                i = prep_state["cast_i"]
                if i >= len(prep_cast_steps):
                    return
                fn, nm = prep_cast_steps[i]
                v = b.dma("gpsimd", g_c[nm], fn)
                if g_c[nm].n == 256:
                    cast_done[nm] = v
                prep_state["cast_i"] = i + 1

        b.wait("vector", s_cn, consts_ld)
        b.emit("vector", lambda e: e.tensor_copy(
            fB[:, 0:128], i128[:]).then_inc(dve.h, 1))
        idf_dve = dve.inc()
        xbt_free = NONE
        p1_bank_free = [NONE, NONE]
        fa1_free = [NONE, NONE]
        for t in range(NC):
            slot = t % 2
            b.waitp("sync", fa1_free[slot])
            ld = b.dma("sync", s_fa[slot], lambda e, t=t, slot=slot: e.dma_start(
                fA[:, 2048 * slot:2048 * slot + 2048], x_d[128 * t:128 * t + 128, :]))
            if t < 8:
                emit_prep_casts(2)
            b.wait("tensor", s_fa[slot], ld)
            b.wait("tensor", s_cn, consts_ld)
            b.wait("tensor", dve, idf_dve)
            for q4 in range(4):
                tq = q4 % 2
                bank = 1024 + 512 * tq
                b.waitp("tensor", p1_bank_free[tq])
                for kq in range(4):
                    c = 4 * q4 + kq
                    last = (kq == 3)
                    def mmf(e, c=c, bank=bank, kq=kq, last=last, slot=slot):
                        mm = e.matmul(pbB[:, bank + 128 * kq:bank + 128 * kq + 128],
                                      fA[:, 2048 * slot + 128 * c:2048 * slot + 128 * c + 128],
                                      fB[:, 0:128], start=True, stop=True)
                        if last:
                            mm.then_inc(pe.h, 1)
                    b.emit("tensor", mmf)
                q_pe = pe.inc()
                b.wait("scalar", pe, q_pe)
                def evf(e, q4=q4, t=t, bank=bank):
                    dst = bigA[:].rearrange("p (c q) -> p c q", c=16)[
                        :, 4 * q4:4 * q4 + 4, 128 * t:128 * t + 128]
                    e.copy(dst, pbB[:, bank:bank + 512]
                           .rearrange("p (c q) -> p c q", c=4)).then_inc(act.h, 1)
                b.emit("scalar", evf)
                p1_bank_free[tq] = (act, act.inc())
            fa1_free[slot] = (pe, pe.n)
        xT_done_act = act.n

        wv_free = NONE
        btmp_free = [NONE] * 4
        v_bank_free = [NONE] * 4
        for nnc in range(4):
            b.waitp("gpsimd", wv_free)
            def wvf(e, nnc=nnc):
                src = wqkv_d[:, 2 * D + 512 * nnc:2 * D + 512 * nnc + 512]
                return e.dma_start(warena[:].rearrange("p (c n) -> p c n", c=16),
                                   src.rearrange("(c p) n -> p c n", p=128))
            ldv = b.dma("gpsimd", g_wv, wvf)
            emit_prep_casts(2)
            emit_w13q(6)
            b.wait("tensor", g_wv, ldv)
            b.wait("tensor", act, xT_done_act)
            for t in range(NC):
                bk = t % 4
                bank = 512 * bk
                b.waitp("tensor", v_bank_free[bk])
                for c in range(NC):
                    last = (c == NC - 1)
                    def mmf(e, t=t, c=c, bank=bank, last=last):
                        mm = e.matmul(pbA[:, bank:bank + 512],
                                      bigA[:, 2048 * c + 128 * t:2048 * c + 128 * t + 128],
                                      warena[:, 512 * c:512 * c + 512],
                                      start=(c == 0), stop=last)
                        if last:
                            mm.then_inc(pe.h, 1)
                    b.emit("tensor", mmf)
                v_pe = pe.inc()
                b.waitp("vector", btmp_free[bk])
                b.wait("vector", pe, v_pe)
                b.emit("vector", lambda e, bk=bk, bank=bank: e.tensor_copy(
                    btmp[:, 512 * bk:512 * bk + 512],
                    pbA[:, bank:bank + 512]).then_inc(dve.h, 1))
                ev_dve = dve.inc()
                v_bank_free[bk] = (dve, ev_dve)
                b.wait("gpsimd", dve, ev_dve)
                stv = b.dma("gpsimd", t_bt[bk], lambda e, t=t, nnc=nnc, bk=bk: e.dma_start(
                    vsc_d[t][:, 512 * nnc:512 * nnc + 512],
                    btmp[:, 512 * bk:512 * bk + 512]))
                btmp_free[bk] = (t_bt[bk], stv)
            wv_free = (pe, pe.n)
        pbA_free = (dve, dve.n)
        vsc_totals = [(s, s.n) for s in t_bt]

        wqk_free = [wv_free, wv_free]
        vsl_free = [NONE, NONE]
        outt_free = [NONE, NONE]
        pt_free = NONE
        vh_lds = {}
        ldqks = {}
        exp_free = [pbA_free] * 4
        pv_free = [NONE, NONE]
        sum_free = [NONE, NONE]
        qk_free = NONE
        kres_free = NONE

        for h in range(NH):
            wslot = (h // 2) % 2
            arena = warena if wslot == 0 else fAb
            if h % 2 == 0:
                hp = h // 2
                b.waitp("gpsimd", wqk_free[wslot])
                def wqf(e, hp=hp, arena=arena):
                    src = wqkv_d[:, 256 * hp:256 * hp + 256]
                    return e.dma_start(
                        arena[:, 0:4096].rearrange("p (c d) -> p c d", c=16),
                        src.rearrange("(c p) d -> p c d", p=128))
                def wkf(e, hp=hp, arena=arena):
                    src = wqkv_d[:, D + 256 * hp:D + 256 * hp + 256]
                    return e.dma_start(
                        arena[:, 4096:8192].rearrange("p (c d) -> p c d", c=16),
                        src.rearrange("(c p) d -> p c d", p=128))
                b.dma("gpsimd", g_qk[wslot], wqf)
                ldqks[h // 2] = b.dma("gpsimd", g_qk[wslot], wkf)
            ldqk = ldqks[h // 2]

            b.wait("tensor", g_qk[wslot], ldqk)
            for pr in exp_free + pv_free + sum_free + [kres_free]:
                b.waitp("tensor", pr)
            hoff = 128 * (h % 2)
            for dst_ps, woff in ((pbA, hoff), (pbB, 4096 + hoff)):
                for c in range(NC):
                    for nn in range(4):
                        last = (c == NC - 1)
                        def mmf(e, dst_ps=dst_ps, woff=woff, nn=nn, c=c, last=last,
                                arena=arena):
                            mm = e.matmul(dst_ps[:, 512 * nn:512 * nn + 512],
                                          arena[:, woff + 256 * c:woff + 256 * c + 128],
                                          bigA[:, 2048 * c + 512 * nn:2048 * c + 512 * nn + 512],
                                          start=(c == 0), stop=last)
                            if last and nn == 3:
                                mm.then_inc(pe.h, 1)
                        b.emit("tensor", mmf)
                pe.inc()
            qkv_mm_pe = pe.n
            b.waitp("scalar", qk_free)
            b.wait("scalar", pe, qkv_mm_pe)
            b.emit("scalar", lambda e: e.copy(qk[:, 0:2048], pbA[:, 0:2048]).then_inc(act.h, 1))
            act.inc()
            b.emit("scalar", lambda e: e.copy(qk[:, 2048:4096], pbB[:, 0:2048]).then_inc(act.h, 1))
            qk_ready = act.inc()
            if h % 2 == 1:
                wqk_free[wslot] = (pe, qkv_mm_pe)
            exp_free = [(act, qk_ready)] * 4
            pv_free = [(act, qk_ready), (act, qk_ready)]
            sum_free = [(act, qk_ready), (act, qk_ready)]

            vslot = h % 2

            def emit_vhead(hh):
                vs = hh % 2
                for s_, v_ in vsc_totals:
                    b.wait("sync", s_, v_)
                b.waitp("sync", vsl_free[vs])
                def vhf(e, hh=hh, vs=vs):
                    src = vsc_d[:, :, 128 * hh:128 * hh + 128].rearrange("j p d -> p j d")
                    return e.dma_start(vsl[:, 2048 * vs:2048 * vs + 2048]
                                       .rearrange("p (j d) -> p j d", j=16), src)
                vh_lds[hh] = b.dma("sync", s_vs[vs], vhf)

            if h == 0:
                emit_vhead(0)
            if h + 1 < NH:
                emit_vhead(h + 1)
            vh_ld = vh_lds[h]

            if h < 12:
                emit_prep_casts(2)
            emit_w13q(8 if h < 10 else 5)
            if h >= 12:
                emit_w2q(8)

            last_tt_dve = 0
            for qp in range(2):
                qcs = (2 * qp, 2 * qp + 1)
                njv = 4 * qcs[1] + 4
                b.wait("tensor", act, qk_ready)
                b.waitp("tensor", pt_free)
                b.wait("tensor", s_vs[vslot], vh_ld)
                b.waitp("scalar", pt_free)
                exp_acts = {}
                for j in range(njv):
                    live = [m for m in range(2) if j <= 4 * qcs[m] + 3]
                    for m in live:
                        qc = qcs[m]
                        sbank = (2 * j + m) % 4
                        b.waitp("tensor", exp_free[sbank])
                        def smf(e, j=j, qc=qc, sbank=sbank):
                            e.matmul(pbA[:, 512 * sbank:512 * sbank + 512],
                                     qk[:, 2048 + 128 * j:2048 + 128 * j + 128],
                                     qk[:, 512 * qc:512 * qc + 512],
                                     start=True, stop=True).then_inc(pe.h, 1)
                        b.emit("tensor", smf)
                        s_pe = pe.inc()
                        r = j - 4 * qc
                        if 0 <= r <= 3:
                            b.waitp("vector", exp_free[sbank])
                            b.wait("vector", pe, s_pe)
                            b.emit("vector", lambda e, sbank=sbank, r=r: e.tensor_add(
                                pbA[:, 512 * sbank:512 * sbank + 512],
                                pbA[:, 512 * sbank:512 * sbank + 512],
                                maskt[:, 512 * r:512 * r + 512]).then_inc(dve.h, 1))
                            mk = dve.inc()
                            b.wait("scalar", dve, mk)
                        else:
                            b.wait("scalar", pe, s_pe)
                        def expf(e, j=j, m=m, sbank=sbank):
                            e.activation(pt[:, 1024 * j + 512 * m:1024 * j + 512 * m + 512],
                                         pbA[:, 512 * sbank:512 * sbank + 512],
                                         AF.Exp, scale=SCALE).then_inc(act.h, 1)
                        b.emit("scalar", expf)
                        exp_free[sbank] = (act, act.inc())
                        exp_acts[(j, m)] = act.n
                for m in range(2):
                    b.waitp("tensor", pv_free[m])
                    b.waitp("tensor", sum_free[m])
                for j in range(njv):
                    for m in range(2):
                        if j > 4 * qcs[m] + 3:
                            continue
                        b.wait("tensor", act, exp_acts[(j, m)])
                        nj_m = 4 * qcs[m] + 4
                        def pvf(e, j=j, m=m, vslot=vslot, last=(j == 4 * qcs[m] + 3)):
                            mm = e.matmul(pbB[:, 512 * m:512 * m + 512],
                                          vsl[:, 2048 * vslot + 128 * j:2048 * vslot + 128 * j + 128],
                                          pt[:, 1024 * j + 512 * m:1024 * j + 512 * m + 512],
                                          start=(j == 0), stop=last)
                            if last:
                                mm.then_inc(pe.h, 1)
                        b.emit("tensor", pvf)
                        if j == 4 * qcs[m] + 3:
                            pv_pes = pe.inc()
                        def suf(e, j=j, m=m, last=(j == 4 * qcs[m] + 3)):
                            mm = e.matmul(pbB[:, 1024 + 512 * m:1536 + 512 * m],
                                          ones128[:],
                                          pt[:, 1024 * j + 512 * m:1024 * j + 512 * m + 512],
                                          start=(j == 0), stop=last)
                            if last:
                                mm.then_inc(pe.h, 1)
                        b.emit("tensor", suf)
                        if j == 4 * qcs[m] + 3:
                            sum_pes = pe.inc()
                            qc = qcs[m]
                            b.wait("vector", pe, sum_pes)
                            b.emit("vector", lambda e, m=m: e.reciprocal(
                                fB[:, 512 * m:512 * m + 512],
                                pbB[:, 1024 + 512 * m:1536 + 512 * m]).then_inc(dve.h, 1))
                            rc_dve = dve.inc()
                            sum_free[m] = (dve, rc_dve)
                            b.wait("vector", dve, rc_dve)
                            b.wait("vector", pe, pv_pes)
                            oslot = h % 2
                            if qc == 0:
                                b.waitp("vector", outt_free[oslot])
                            b.emit("vector", lambda e, m=m, qc=qc, oslot=oslot: e.tensor_mul(
                                outt[:, 2048 * oslot + 512 * qc:2048 * oslot + 512 * qc + 512],
                                pbB[:, 512 * m:512 * m + 512],
                                fB[:, 512 * m:512 * m + 512]).then_inc(dve.h, 1))
                            last_tt_dve = dve.inc()
                            pv_free[m] = (dve, last_tt_dve)
                pt_free = (pe, pe.n)
            b.wait("sync", dve, last_tt_dve)
            sto_t = b.dma("sync", t_ot[h % 2], lambda e, h=h, oslot=h % 2: e.dma_start(
                att_d[h], outt[:, 2048 * oslot:2048 * oslot + 2048]))
            outt_free[h % 2] = (t_ot[h % 2], sto_t)
            qk_free = (pe, pe.n)
            vsl_free[vslot] = (pe, pe.n)
            kres_free = (dve, last_tt_dve)

        pbA_free = (act, act.n)
        att_done_dve = dve.n
        att_pe_done = pe.n
        att_totals = [(s, s.n) for s in t_ot]
        emit_w13q(999)
        assert w13_state["i"] == len(w13q_steps), "w13 prep not fully emitted"
        _w13_stf(len(w13q_steps) - 2)
        _w13_stf(len(w13q_steps) - 1)
        w13_done = [(t_qst[0], t_qst[0].n), (t_qst[1], t_qst[1].n)]
        _w2_stf_upto(10**9)
        w2_sofar = [(t_q2s[0], t_q2s[0].n), (t_q2s[1], t_q2s[1].n)]
        assert prep_state["cast_i"] == len(prep_cast_steps)

        b.wait("gpsimd", pe, att_pe_done)
        for hh in range(NC):
            b.dma("gpsimd", g_wo, lambda e, hh=hh: e.dma_start(
                bigA[:, 2048 * hh:2048 * hh + 2048],
                wout_d[128 * hh:128 * hh + 128, :]))
        wout_ld = g_wo.n

        b.wait("tensor", s_cn, consts_ld)
        b.waitp("tensor", pbA_free)
        for nn in range(4):
            last = (nn == 3)
            def mmf(e, nn=nn, last=last):
                mm = e.matmul(pbA[:, 512 * nn:512 * nn + 512], onesf[:],
                              b_sb[:, 512 * nn:512 * nn + 512], start=True, stop=True)
                if last:
                    mm.then_inc(pe.h, 1)
            b.emit("tensor", mmf)
        bbc_pe = pe.inc()
        b.wait("vector", pe, bbc_pe)
        for s_, v_ in w2_sofar:
            b.wait("vector", s_, v_)
        b.emit("vector", lambda e: e.tensor_copy(b_bc[:], pbA[:, 0:2048]).then_inc(dve.h, 1))
        bbc_dve = dve.inc()

        fA_free = [wqk_free[1], wqk_free[1]]
        fB_free = [(dve, att_done_dve), NONE]
        proj_ps_free = [(dve, bbc_dve), (act, act.n)]
        proj_lds = {}

        def emit_proj_loads(t):
            tslot = t % 2
            for s_, v_ in att_totals:
                b.wait("sync", s_, v_)
            b.waitp("sync", vsl_free[tslot])
            def asl(e, t=t, tslot=tslot):
                src = att_d[:, :, 128 * t:128 * t + 128].rearrange("h p q -> p h q")
                return e.dma_start(vsl[:, 2048 * tslot:2048 * tslot + 2048]
                                   .rearrange("p (h q) -> p h q", h=16), src)
            asl_ld = b.dma("sync", s_vs[tslot], asl)
            b.waitp("sync", fA_free[tslot])
            xres_ld = b.dma("sync", s_fa[tslot], lambda e, t=t, tslot=tslot: e.dma_start(
                fA[:, 2048 * tslot:2048 * tslot + 2048],
                x_d[128 * t:128 * t + 128, :]))
            proj_lds[t] = (asl_ld, xres_ld)

        emit_proj_loads(0)
        for t in range(NC):
            tslot = t % 2
            PS = pbA if tslot == 0 else pbB
            if t + 1 < NC:
                emit_proj_loads(t + 1)
            asl_ld, xres_ld = proj_lds[t]

            b.wait("tensor", g_wo, wout_ld)
            b.wait("tensor", s_vs[tslot], asl_ld)
            b.waitp("tensor", proj_ps_free[tslot])
            for hh in range(NC):
                for nn in range(4):
                    last = (hh == NC - 1)
                    def mmf(e, PS=PS, tslot=tslot, hh=hh, nn=nn, last=last):
                        mm = e.matmul(PS[:, 512 * nn:512 * nn + 512],
                                      vsl[:, 2048 * tslot + 128 * hh:2048 * tslot + 128 * hh + 128],
                                      bigA[:, 2048 * hh + 512 * nn:2048 * hh + 512 * nn + 512],
                                      start=(hh == 0), stop=last)
                        if last and nn == 3:
                            mm.then_inc(pe.h, 1)
                    b.emit("tensor", mmf)
            proj_pe = pe.inc()
            vsl_free[tslot] = (pe, proj_pe)
            b.wait("vector", pe, proj_pe)
            b.wait("vector", s_fa[tslot], xres_ld)
            b.wait("vector", dve, bbc_dve)
            b.waitp("vector", fB_free[0])
            b.emit("vector", lambda e, PS=PS: e.tensor_add(
                fB[:, 0:2048], PS[:, 0:2048], b_bc[:]).then_inc(dve.h, 1))
            proj_ps_free[tslot] = (dve, dve.inc())
            b.emit("vector", lambda e, tslot=tslot: e.tensor_add(
                fB[:, 0:2048], fB[:, 0:2048],
                fA[:, 2048 * tslot:2048 * tslot + 2048]).then_inc(dve.h, 1))
            dve.inc()
            fA_free[tslot] = (dve, dve.n)
            if debug:
                b.wait("sync", dve, dve.n)
                b.dma("sync", t_dbg, lambda e, t=t: e.dma_start(dbg_y1_d[t], fB[:, 0:2048]))
                b.wait("vector", t_dbg, t_dbg.n)
            emit_ln()
            b.wait("sync", dve, dve.n)
            b.dma("sync", t_fb, lambda e, t=t: e.dma_start(x1sc_d[t], fB[:, 0:2048]))
            fB_free[0] = (t_fb, t_fb.n)

        w2_guard["pr"] = (dve, dve.n)

        x1_store_done = t_fb.n
        hid_free = (pe, pe.n)
        barrier = (dve, dve.n)
        w13_free = [NONE, NONE]
        w2_free = [NONE] * 8
        sg_free = [NONE, NONE]
        gu_bank_free = [barrier] * 4
        sa_bank_free = [barrier, barrier]
        w2_ps_free = [barrier, barrier]
        x1t_free = NONE
        first_chunk = True

        def emit_x1t_build(kb):
            nonlocal x1t_free, first_chunk, xbt_free
            for tl in range(4):
                t = 4 * kb + tl
                b.wait("gpsimd", t_fb, x1_store_done)
                b.waitp("gpsimd", xbt_free)
                x1_ld = b.dma("gpsimd", g_xb[0], lambda e, t=t: e.dma_start(
                    xbt[:, 0:2048], x1sc_d[t]))
                b.wait("tensor", g_xb[0], x1_ld)
                b.waitp("tensor", x1t_free)
                for q4 in range(4):
                    tq = q4 % 2
                    b.waitp("tensor", sa_bank_free[tq])
                    for kq in range(4):
                        c = 4 * q4 + kq
                        last = (kq == 3)
                        def mmf(e, c=c, tq=tq, kq=kq, last=last):
                            mm = e.matmul(pbB[:, 512 * tq + 128 * kq:512 * tq + 128 * kq + 128],
                                          xbt[:, 128 * c:128 * c + 128],
                                          i128[:], start=True, stop=True)
                            if last:
                                mm.then_inc(pe.h, 1)
                        b.emit("tensor", mmf)
                    tq_pe = pe.inc()
                    if first_chunk:
                        for s_, v_ in w13_done:
                            b.wait("scalar", s_, v_)
                        first_chunk = False
                    b.wait("scalar", pe, tq_pe)
                    def evf(e, q4=q4, tq=tq, tl=tl):
                        dst = bigB[:].rearrange("p (c x) -> p c x", c=16)[
                            :, 4 * q4:4 * q4 + 4, 128 * tl:128 * tl + 128]
                        e.copy(dst, pbB[:, 512 * tq:512 * tq + 512]
                               .rearrange("p (c x) -> p c x", c=4)).then_inc(act.h, 1)
                    b.emit("scalar", evf)
                    sa_bank_free[tq] = (act, act.inc())
                xbt_free = (pe, pe.n)

        emit_x1t_build(0)
        x1t_act = act.n
        bf_state = {"ct": 0, "slot": [0] * 4, "tile": [0] * NJ, "fin": []}

        w2_lds = {}
        NW2 = 8 * NJ

        def emit_w2_load(g):
            if g >= NW2:
                return
            slot = g % 8
            jt = W2_ORDER[g % NJ]
            q = "sync" if slot % 2 == 0 else "gpsimd"
            b.waitp(q, w2_ts_tg[jt])
            b.waitp(q, w2_free[slot])
            w2_lds[g] = b.dma(q, s_w2[slot], lambda e, j=jt, slot=slot:
                              e.dma_start(pt[:, 2048 * slot:2048 * slot + 2048],
                                          w2t_d[j]))

        w2_free = [(pe, pe.n)] * 8
        for gp in range(8):
            emit_w2_load(gp)

        for kk in range(4):
            bf_state["fin"] = []
            lvl = [7] * NJ
            pairs = []
            for s in range(5, -1, -1):
                stp = 1 << s
                for base in range(0, NJ, 2 * stp):
                    for o in range(stp):
                        pairs.append([s, base + o, base + o + stp, False])

            def emit_ready_pairs(cap=None):
                ct = 0
                progress = True
                while progress:
                    progress = False
                    for pr in pairs:
                        if cap is not None and ct >= cap:
                            return
                        s, a_, b_, done = pr
                        if done or lvl[a_] != s + 1 or lvl[b_] != s + 1:
                            continue
                        a0, b0 = 512 * a_, 512 * b_
                        fin = (s == 0)
                        bs = bf_state["ct"] % 4
                        bf_state["ct"] += 1
                        b.wait("vector", act, bf_state["slot"][bs])
                        b.wait("vector", act, max(bf_state["tile"][a_],
                                                  bf_state["tile"][b_]))
                        b.emit("vector", lambda e, a0=a0, b0=b0, bs=bs: e.tensor_sub(
                            btmp[:, 512 * bs:512 * bs + 512],
                            bigA[:, a0:a0 + 512],
                            bigA[:, b0:b0 + 512]).then_inc(dve.h, 1))
                        sub_dve = dve.inc()
                        def addf(e, a0=a0, b0=b0, fin=fin):
                            mm = e.tensor_add(bigA[:, a0:a0 + 512],
                                              bigA[:, a0:a0 + 512], bigA[:, b0:b0 + 512])
                            if fin:
                                mm.then_inc(dve.h, 1)
                        b.emit("vector", addf)
                        add_tg = dve.inc() if fin else None
                        b.wait("scalar", dve, sub_dve)
                        b.emit("scalar", lambda e, b0=b0, bs=bs: e.copy(
                            bigA[:, b0:b0 + 512],
                            btmp[:, 512 * bs:512 * bs + 512]).then_inc(act.h, 1))
                        cp_act = act.inc()
                        bf_state["slot"][bs] = cp_act
                        bf_state["tile"][b_] = cp_act
                        if fin:
                            bf_state["fin"].append((a_, add_tg, b_, cp_act))
                        pr[3] = True
                        lvl[a_] = s
                        lvl[b_] = s
                        ct += 1
                        progress = True

            mult_dves = {}
            for ji in range(NJ):
                if kk == 0:
                    emit_w2q(1)
                j = J_ARRIVE[ji]
                jslot = ji % 2
                for s_, v_ in w13_done:
                    b.wait("sync", s_, v_)
                b.waitp("sync", w13_free[jslot])
                b.dma("sync", s_w13[jslot], lambda e, j=j, jslot=jslot: e.dma_start(
                    warena[:, 4096 * jslot:4096 * jslot + 2048], w1t_d[j]))
                for s_, v_ in w13_done:
                    b.wait("gpsimd", s_, v_)
                b.waitp("gpsimd", w13_free[jslot])
                slab_ld = b.dma("gpsimd", s_w13[jslot], lambda e, j=j, jslot=jslot: e.dma_start(
                    warena[:, 4096 * jslot + 2048:4096 * jslot + 4096], w3t_d[j]))

                b.wait("tensor", s_w13[jslot], slab_ld)
                b.wait("tensor", act, x1t_act)
                for woff, bank in ((4096 * jslot, 512 * jslot),
                                   (4096 * jslot + 2048, 1024 + 512 * jslot)):
                    b.waitp("tensor", gu_bank_free[bank // 512])
                    for c in range(NC):
                        last = (c == NC - 1)
                        def mmf(e, woff=woff, bank=bank, c=c, last=last):
                            mm = e.matmul(pbA[:, bank:bank + 512],
                                          warena[:, woff + 128 * c:woff + 128 * c + 128],
                                          bigB[:, 512 * c:512 * c + 512],
                                          start=(c == 0), stop=last)
                            if last:
                                mm.then_inc(pe.h, 1)
                        b.emit("tensor", mmf)
                    pe.inc()
                gu_pe = pe.n
                w13_free[jslot] = (pe, gu_pe)
                sslot = ji % 2
                b.waitp("scalar", sg_free[sslot])
                b.wait("scalar", pe, gu_pe)
                b.emit("scalar", lambda e, sslot=sslot, jslot=jslot: e.activation(
                    sg[:, 512 * sslot:512 * sslot + 512],
                    pbA[:, 512 * jslot:512 * jslot + 512], AF.Silu).then_inc(act.h, 1))
                silu_act = act.inc()
                gu_bank_free[jslot] = (act, silu_act)
                if j == 0:
                    b.waitp("vector", hid_free)
                b.wait("vector", act, silu_act)
                b.emit("vector", lambda e, j=j, sslot=sslot, jslot=jslot: e.tensor_mul(
                    bigA[:, 512 * j:512 * j + 512],
                    pbA[:, 1024 + 512 * jslot:1024 + 512 * jslot + 512],
                    sg[:, 512 * sslot:512 * sslot + 512]).then_inc(dve.h, 1))
                mult_dve = dve.inc()
                gu_bank_free[2 + jslot] = (dve, mult_dve)
                sg_free[sslot] = (dve, mult_dve)
                mult_dves[j] = mult_dve
                jds_l = [J_ARRIVE[ji - 1]] if ji > 0 else []
                if ji == NJ - 1:
                    jds_l.append(J_ARRIVE[ji])
                for pos, jd in enumerate(jds_l):
                    jds = (ji - 1 + pos) % 2
                    b.wait("tensor", dve, mult_dves[jd])
                    b.waitp("tensor", sa_bank_free[jds])
                    def mmf(e, jd=jd, jds=jds):
                        e.matmul(pbB[:, 512 * jds:512 * jds + 512], h128[:],
                                 bigA[:, 512 * jd:512 * jd + 512],
                                 start=True, stop=True).then_inc(pe.h, 1)
                    b.emit("tensor", mmf)
                    sa_pe = pe.inc()
                    b.wait("vector", pe, sa_pe)
                    b.emit("vector", lambda e, jd=jd, jds=jds: e.tensor_scalar_mul(
                        bigA[:, 512 * jd:512 * jd + 512],
                        pbB[:, 512 * jds:512 * jds + 512], WHT_SCALE).then_inc(dve.h, 1))
                    sa_bank_free[jds] = (dve, dve.inc())
                    lvl[jd] = 6
                    emit_ready_pairs(cap=None if ji == NJ - 1 else 3)
            x1t_free = (pe, pe.n)
            assert all(pr[3] for pr in pairs)
            if kk == 0:
                emit_w2q(999)
                assert w2_state["i"] == len(w2q_steps)
                _w2_stf_upto(10**9)
                w2_all = [(t_q2s[0], t_q2s[0].n), (t_q2s[1], t_q2s[1].n)]
            order = []
            tile_wait = {}
            for a_, dtg, b_, atg in bf_state["fin"]:
                order.append(a_)
                tile_wait[a_] = (dve, dtg)
                order.append(b_)
                tile_wait[b_] = (act, atg)
            assert order == W2_ORDER

            hid_ready = dve.n
            hid_act = act.n
            x1t_act_next = None
            if kk < 3:
                emit_x1t_build(kk + 1)
                x1t_act_next = act.n
            if debug and kk == 1:
                b.wait("sync", dve, hid_ready)
                b.wait("sync", act, hid_act)
                b.dma("sync", t_dbg, lambda e: e.dma_start(dbg_hid_d, bigA[:]))
                b.wait("tensor", t_dbg, t_dbg.n)

            for tc2 in range(2):
                b.waitp("tensor", w2_ps_free[0])
                b.waitp("tensor", w2_ps_free[1])
                if x1t_act_next is not None:
                    b.wait("tensor", act, x1t_act_next)
                gbase = (kk * 2 + tc2) * NJ
                x1r_lds = {}

                def emit_x1r_loads():
                    for t2 in range(2):
                        tg = 4 * kk + 2 * tc2 + t2
                        b.wait("sync", t_fb, x1_store_done)
                        b.waitp("sync", fA_free[t2])
                        x1r_lds[t2] = b.dma("sync", s_fa[t2], lambda e, tg=tg, aslot=t2:
                                            e.dma_start(fA[:, 2048 * aslot:2048 * aslot + 2048],
                                                        x1sc_d[tg]))

                for idx in range(NJ):
                    if idx == 40:
                        emit_x1r_loads()
                    tile = W2_ORDER[idx]
                    wsem, wtg = tile_wait[tile]
                    b.wait("tensor", wsem, wtg)
                    j4 = (gbase + idx) % 8
                    b.wait("tensor", s_w2[j4], w2_lds[gbase + idx])
                    for t2 in range(2):
                        PS = pbA if t2 == 0 else pbB
                        tloc = 2 * tc2 + t2
                        for nn in range(4):
                            last_j = (idx == NJ - 1)
                            last_mm = (t2 == 1 and nn == 3)
                            def mmf(e, PS=PS, tile=tile, j4=j4, tloc=tloc, nn=nn,
                                    idx=idx, last_j=last_j, last_mm=last_mm):
                                mm = e.matmul(PS[:, 512 * nn:512 * nn + 512],
                                              bigA[:, 512 * tile + 128 * tloc:512 * tile + 128 * tloc + 128],
                                              pt[:, 2048 * j4 + 512 * nn:2048 * j4 + 512 * nn + 512],
                                              start=(idx == 0), stop=last_j)
                                if last_mm:
                                    mm.then_inc(pe.h, 1)
                            b.emit("tensor", mmf)
                    w2_free[j4] = (pe, pe.inc())
                    emit_w2_load(gbase + idx + 8)
                w2_pe = pe.n
                for t2 in range(2):
                    PS = pbA if t2 == 0 else pbB
                    dst = fB[:, 0:2048] if t2 == 0 else b_bc[:]
                    if kk == 0 and t2 == 1:
                        for s_, v_ in w2_all:
                            b.wait("vector", s_, v_)
                    b.wait("vector", pe, w2_pe)
                    b.wait("vector", s_fa[t2], x1r_lds[t2])
                    b.waitp("vector", fB_free[t2])
                    b.emit("vector", lambda e, PS=PS, dst=dst, t2=t2: e.tensor_add(
                        dst, PS[:, 0:2048],
                        fA[:, 2048 * t2:2048 * t2 + 2048]).then_inc(dve.h, 1))
                    w2_ps_free[t2] = (dve, dve.inc())
                    fA_free[t2] = (dve, dve.n)
                for t2 in range(2):
                    tg = 4 * kk + 2 * tc2 + t2
                    dst = fB[:, 0:2048] if t2 == 0 else b_bc[:]
                    emit_ln(None if t2 == 0 else b_bc[:])
                    b.wait("sync", dve, dve.n)
                    b.dma("sync", t_fb, lambda e, tg=tg, dst=dst: e.dma_start(
                        y_d[128 * tg:128 * tg + 128, :], dst))
                    fB_free[t2] = (t_fb, t_fb.n)
            hid_free = (pe, pe.n)
            gu_bank_free = [w2_ps_free[0]] * 2 + [w2_ps_free[1]] * 2
            sa_bank_free = [w2_ps_free[1], w2_ps_free[1]]
            w13_free = [(pe, pe.n), (pe, pe.n)]
            if x1t_act_next is not None:
                x1t_act = x1t_act_next

        b.emit("sync", lambda e, v=t_fb.n: e.wait_ge(t_fb.h, v))

        with nc.Block() as block:
            @block.sync
            def _(e):
                for fn in b.ops["sync"]:
                    fn(e)

            @block.gpsimd
            def _(e):
                for fn in b.ops["gpsimd"]:
                    fn(e)

            @block.tensor
            def _(e):
                for fn in b.ops["tensor"]:
                    fn(e)

            @block.scalar
            def _(e):
                for fn in b.ops["scalar"]:
                    fn(e)

            @block.vector
            def _(e):
                for fn in b.ops["vector"]:
                    fn(e)

    counts = {e: len(b.ops[e]) for e in ENGINES}
    return nc, counts


_CACHE = {}


def _get_program():
    if "nc" not in _CACHE:
        nc, counts = build()
        _CACHE["nc"] = nc
        _CACHE["counts"] = counts
    return _CACHE["nc"], _CACHE["counts"]


def _consts():
    i128 = np.eye(128, dtype=BF)
    idx = np.arange(128)
    pop = np.zeros((128, 128), dtype=np.int64)
    for bit in range(7):
        pop += ((idx[:, None] >> bit) & 1) * ((idx[None, :] >> bit) & 1)
    h128 = np.where(pop % 2 == 0, 1.0, -1.0).astype(BF)
    p = idx[:, None]
    f = np.arange(512)[None, :]
    regs = [np.where(128 * r + p > f, -1e30, 0.0) for r in range(4)]
    maskt = np.concatenate(regs, axis=1).astype(BF)
    onesb = np.ones((128, 128), dtype=BF)
    onesf = np.ones((1, 128), dtype=np.float32)
    return i128, h128, maskt, onesb, onesf


def kernel(L0_attn_x, L0_attn_w_qkv, L0_attn_w_out, L0_attn_b_out,
           L0_ffn_W1, L0_ffn_W2, L0_ffn_W3):
    nc, _ = _get_program()
    i128, h128, maskt, onesb, onesf = _consts()
    x = np.ascontiguousarray(np.asarray(L0_attn_x, dtype=np.float32))
    wqkv = np.ascontiguousarray(np.asarray(L0_attn_w_qkv, dtype=np.float32))
    wout = np.ascontiguousarray(np.asarray(L0_attn_w_out, dtype=np.float32))
    bout = np.ascontiguousarray(np.asarray(L0_attn_b_out, dtype=np.float32)).reshape(1, D)
    W1 = np.ascontiguousarray(np.asarray(L0_ffn_W1, dtype=np.float32))
    W2 = np.ascontiguousarray(np.asarray(L0_ffn_W2, dtype=np.float32))
    W3 = np.ascontiguousarray(np.asarray(L0_ffn_W3, dtype=np.float32))
    in_maps = []
    for c in range(8):
        in_maps.append({
            "x": np.ascontiguousarray(x[c]),
            "w_qkv": wqkv, "w_out": wout, "b_out": bout,
            "W1": W1, "W2": W2, "W3": W3,
            "i128": i128, "h128": h128, "maskt": maskt, "onesb": onesb,
            "onesf": onesf,
        })
    res = run_bass_kernel_spmd(nc, in_maps, core_ids=list(range(8)))
    out = np.stack([np.asarray(r["y"], dtype=np.float32) for r in res.results], axis=0)
    return out


if __name__ == "__main__":
    nc, counts = _get_program()
    print("instruction list sizes:", counts, "total:", sum(counts.values()))


# version 33
# speedup vs baseline: 1.1062x; 1.0375x over previous
import math
import numpy as np
import ml_dtypes
from contextlib import ExitStack

import concourse.bass as bass
import concourse.mybir as mybir
from concourse.bass_utils import run_bass_kernel_spmd

F32 = mybir.dt.float32
BF16 = mybir.dt.bfloat16
AF = mybir.ActivationFunctionType
ALU = mybir.AluOpType
AX = mybir.AxisListType
BF = ml_dtypes.bfloat16

T = 2048
D = 2048
NH = 16
DH = 128
HF = 8192
NC = 16
NJ = 64
SCALE = 1.0 / math.sqrt(DH)
WHT_SCALE = 1.0 / math.sqrt(HF)
EPS = 1e-5

ENGINES = ("sync", "gpsimd", "tensor", "scalar", "vector")


class Sem:
    def __init__(self, stack, nc, name):
        self.h = stack.enter_context(nc.semaphore(name))
        self.name = name
        self.n = 0

    def inc(self, k=1):
        self.n += k
        return self.n


class B:

    def __init__(self):
        self.ops = {e: [] for e in ENGINES}
        self.last_wait = {}

    def emit(self, eng, fn):
        self.ops[eng].append(fn)

    def wait(self, eng, sem, val):
        if sem is None or val <= 0:
            return
        key = (eng, sem.name)
        if self.last_wait.get(key, 0) >= val:
            return
        self.last_wait[key] = val
        self.emit(eng, lambda e, s=sem.h, v=val: e.wait_ge(s, v))

    def waitp(self, eng, pair):
        self.wait(eng, pair[0], pair[1])

    def dma(self, eng, sem, fn):
        self.emit(eng, lambda e, fn=fn, s=sem.h: fn(e).then_inc(s, 16))
        return sem.inc(16)


def build(debug=False):
    nc = bass.Bass("TRN2", target_bir_lowering=False)
    SCR = "ExternalOutput" if debug else "Internal"

    x_d = nc.dram_tensor("x", [T, D], F32, kind="ExternalInput").ap()
    wqkv_d = nc.dram_tensor("w_qkv", [D, 3 * D], F32, kind="ExternalInput").ap()
    wout_d = nc.dram_tensor("w_out", [D, D], F32, kind="ExternalInput").ap()
    bout_d = nc.dram_tensor("b_out", [1, D], F32, kind="ExternalInput").ap()
    w1_d = nc.dram_tensor("W1", [HF, D], F32, kind="ExternalInput").ap()
    w2_d = nc.dram_tensor("W2", [D, HF], F32, kind="ExternalInput").ap()
    w3_d = nc.dram_tensor("W3", [HF, D], F32, kind="ExternalInput").ap()
    i128_d = nc.dram_tensor("i128", [128, 128], BF16, kind="ExternalInput").ap()
    h128_d = nc.dram_tensor("h128", [128, 128], BF16, kind="ExternalInput").ap()
    maskt_d = nc.dram_tensor("maskt", [128, 2048], BF16, kind="ExternalInput").ap()
    onesb_d = nc.dram_tensor("onesb", [128, 128], BF16, kind="ExternalInput").ap()
    onesf_d = nc.dram_tensor("onesf", [1, 128], F32, kind="ExternalInput").ap()
    y_d = nc.dram_tensor("y", [T, D], F32, kind="ExternalOutput").ap()
    w1b_d = nc.dram_tensor("w1b", [HF, D], BF16, kind="Internal").ap()
    w3b_d = nc.dram_tensor("w3b", [HF, D], BF16, kind="Internal").ap()
    w2b_d = nc.dram_tensor("w2b", [D, HF], BF16, kind="Internal").ap()
    w1t_d = nc.dram_tensor("w1t", [NJ, 128, D], BF16, kind="Internal").ap()
    w3t_d = nc.dram_tensor("w3t", [NJ, 128, D], BF16, kind="Internal").ap()
    w2t_d = nc.dram_tensor("w2t", [NJ, 128, D], BF16, kind="Internal").ap()
    vsc_d = nc.dram_tensor("vsc", [NC, 128, D], BF16, kind=SCR).ap()
    att_d = nc.dram_tensor("att", [NH, 128, T], BF16, kind=SCR).ap()
    x1sc_d = nc.dram_tensor("x1sc", [NC, 128, D], F32, kind=SCR).ap()
    dbg_y1_d = dbg_hid_d = None
    if debug:
        dbg_y1_d = nc.dram_tensor("dbg_y1", [NC, 128, D], F32, kind="ExternalOutput").ap()
        dbg_hid_d = nc.dram_tensor("dbg_hid", [128, NJ * 512], BF16, kind="ExternalOutput").ap()

    with ExitStack() as stack:
        def sb(name, cols, dtype=BF16, parts=128):
            return stack.enter_context(nc.sbuf_tensor(name, [parts, cols], dtype))

        bigA = sb("bigA", 16 * 2048)
        bigB = sb("bigB", 8192)
        qk = sb("qk", 4096)
        pt = sb("pt", 16 * 1024)
        outt = sb("outt", 2 * 2048)
        vsl = sb("vsl", 2 * 2048)
        warena = sb("warena", 8192)
        btmp = sb("btmp", 2048)
        sg = sb("sg", 2 * 512)
        xbt = sb("xbt", 2048)
        i128 = sb("i128s", 128)
        h128 = sb("h128s", 128)
        maskt = sb("maskts", 2048)
        ones128 = sb("ones128s", 128)
        onesf = sb("onesfs", 128, F32, parts=1)
        b_sb = sb("b_sb", 2048, F32, parts=1)
        b_bc = sb("b_bc", 2048, F32)
        fA = sb("fA", 2 * 2048, F32)
        fAb = fA.bitcast(BF16)
        fB = sb("fB", 2048, F32)
        st = sb("st", 16, F32)

        pbA = stack.enter_context(nc.psum_tensor("pbA", [128, 2048], F32))
        pbB = stack.enter_context(nc.psum_tensor("pbB", [128, 2048], F32))

        S = lambda name: Sem(stack, nc, name)
        s_cn = S("s_cn")
        g_xb = [S("g_xb0"), S("g_xb1")]
        g_c = {"w1": S("g_c1"), "w3": S("g_c3"), "w2": S("g_c2")}
        g_wv = S("g_wv")
        g_qk = [S("g_qk0"), S("g_qk1")]
        g_wo = S("g_wo")
        s_tr = S("s_tr")
        t_ps = S("t_ps")
        s_qtr = [S("s_qtr0"), S("s_qtr1")]
        t_qst = [S("t_qst0"), S("t_qst1")]
        s_q2r = [S("s_q2r0"), S("s_q2r1")]
        t_q2s = [S("t_q2s0"), S("t_q2s1")]
        t_bt = [S("t_bt0"), S("t_bt1"), S("t_bt2"), S("t_bt3")]
        s_vs = [S("s_vs0"), S("s_vs1")]
        t_ot = [S("t_ot0"), S("t_ot1")]
        s_fa = [S("s_fa0"), S("s_fa1")]
        t_fb = S("t_fb")
        s_w13 = [S("s_w13a"), S("s_w13b")]
        s_w2 = [S("s_w2%d" % i) for i in range(8)]
        t_dbg = S("t_dbg")
        pe = S("pe")
        act = S("act")
        dve = S("dve")

        b = B()
        NONE = (None, 0)

        MX, BIAS, SUM, RC, S1, M, NM, QS, M2, VAR, SD, RS = range(12)

        def stc(i):
            return st[:, i:i + 1]

        def emit_ln(buf=None):
            fBs = fB[:, 0:2048] if buf is None else buf
            def vstat(fn):
                b.emit("vector", lambda e, fn=fn: fn(e).then_inc(dve.h, 1))
                b.wait("vector", dve, dve.inc())
            vstat(lambda e: e.reduce_sum(stc(S1), fBs, axis=AX.X))
            vstat(lambda e: e.tensor_scalar_mul(stc(M), stc(S1), 1.0 / 2048.0))
            vstat(lambda e: e.tensor_mul(btmp[:, 0:2048], fBs, fBs))
            vstat(lambda e: e.reduce_sum(stc(QS), btmp[:, 0:2048], axis=AX.X))
            vstat(lambda e: e.tensor_mul(stc(M2), stc(M), stc(M)))
            vstat(lambda e: e.tensor_scalar_mul(stc(QS), stc(QS), 1.0 / 2048.0))
            vstat(lambda e: e.tensor_sub(stc(VAR), stc(QS), stc(M2)))
            vstat(lambda e: e.tensor_scalar_add(stc(VAR), stc(VAR), EPS))
            b.wait("scalar", dve, dve.n)
            b.emit("scalar", lambda e: e.sqrt(stc(SD), stc(VAR)).then_inc(act.h, 1))
            sd_act = act.inc()
            b.wait("vector", act, sd_act)
            vstat(lambda e: e.reciprocal(stc(RS), stc(SD)))
            vstat(lambda e: e.tensor_scalar_sub(fBs, fBs, stc(M)))
            vstat(lambda e: e.tensor_scalar_mul(fBs, fBs, stc(RS)))

        for dst, src in ((i128, i128_d), (h128, h128_d), (maskt, maskt_d),
                         (ones128, onesb_d), (onesf, onesf_d), (b_sb, bout_d)):
            b.dma("sync", s_cn, lambda e, dst=dst, src=src: e.dma_start(dst[:], src))
        consts_ld = s_cn.n

        J_ARRIVE = [int("{:06b}".format(j)[::-1], 2) for j in range(NJ)]

        def _sim_w2_order():
            lvl = [7] * NJ
            pairs = []
            for s in range(5, -1, -1):
                stp = 1 << s
                for base in range(0, NJ, 2 * stp):
                    for o in range(stp):
                        pairs.append([s, base + o, base + o + stp, False])
            order = []

            def ready(cap=None):
                ct = 0
                progress = True
                while progress:
                    progress = False
                    for pr in pairs:
                        if cap is not None and ct >= cap:
                            return
                        s, a_, b_, done = pr
                        if done or lvl[a_] != s + 1 or lvl[b_] != s + 1:
                            continue
                        if s == 0:
                            order.append(a_)
                            order.append(b_)
                        pr[3] = True
                        lvl[a_] = s
                        lvl[b_] = s
                        ct += 1
                        progress = True

            for ji in range(NJ):
                jds = [J_ARRIVE[ji - 1]] if ji > 0 else []
                if ji == NJ - 1:
                    jds.append(J_ARRIVE[ji])
                for jd in jds:
                    lvl[jd] = 6
                    ready(cap=None if ji == NJ - 1 else 3)
            assert sorted(order) == list(range(NJ))
            return order

        W2_ORDER = _sim_w2_order()

        cast_done = {}
        prep_cast_steps = []
        for nm, src, dst in (("w1", w1_d, w1b_d), ("w3", w3_d, w3b_d)):
            for p in range(16):
                def cf(e, src=src, dst=dst, p=p):
                    return e.dma_start(dst[512 * p:512 * p + 512, :],
                                       src[512 * p:512 * p + 512, :])
                prep_cast_steps.append((cf, nm))
        for p in range(16):
            def cf(e, p=p):
                return e.dma_start(w2b_d[128 * p:128 * p + 128, :],
                                   w2_d[128 * p:128 * p + 128, :])
            prep_cast_steps.append((cf, "w2"))

        bcb = b_bc.bitcast(BF16)
        w13q_steps = []
        for nm, wb_d, wt_d in (("w1", w1b_d, w1t_d), ("w3", w3b_d, w3t_d)):
            for c in range(NC):
                for q in range(4):
                    def trf(e, qs, wb_d=wb_d, c=c, q=q):
                        return e.dma_start(
                            bigB[:, 2048 * qs:2048 * qs + 2048],
                            wb_d[2048 * q:2048 * q + 2048, 128 * c:128 * c + 128],
                            transpose=True)

                    def stf(e, qs, wt_d=wt_d, c=c, q=q):
                        dst = wt_d.rearrange("j p h -> p j h")[
                            :, 16 * q:16 * q + 16, 128 * c:128 * c + 128]
                        src = bigB[:, 2048 * qs:2048 * qs + 2048] \
                            .rearrange("p (j h) -> p j h", j=16)
                        return e.dma_start(dst, src)
                    w13q_steps.append((nm, trf, stf))

        w13_state = {"i": 0}
        w13q_trf_tg = {}

        def _w13_stf(idx):
            qs = idx % 2
            stf = w13q_steps[idx][2]
            b.wait("gpsimd", s_qtr[qs], w13q_trf_tg[idx])
            b.dma("gpsimd", t_qst[qs], lambda e, stf=stf, qs=qs: stf(e, qs))

        def emit_w13q(k):
            for _ in range(k):
                i = w13_state["i"]
                if i >= len(w13q_steps):
                    return
                nm = w13q_steps[i][0]
                if cast_done.get(nm) is None:
                    return
                if i >= 2:
                    _w13_stf(i - 2)
                qs = i % 2
                trf = w13q_steps[i][1]
                b.wait("sync", g_c[nm], cast_done[nm])
                b.wait("sync", t_qst[qs], t_qst[qs].n)
                w13q_trf_tg[i] = b.dma(
                    "sync", s_qtr[qs], lambda e, trf=trf, qs=qs: trf(e, qs))
                w13_state["i"] = i + 1

        w2q_steps = []
        for j in W2_ORDER:
            def trf2(e, qs, j=j):
                return e.dma_start(bcb[:, 2048 * qs:2048 * qs + 2048],
                                   w2b_d[:, 128 * j:128 * j + 128],
                                   transpose=True)

            def stf2(e, qs, j=j):
                return e.dma_start(w2t_d[j], bcb[:, 2048 * qs:2048 * qs + 2048])
            w2q_steps.append((j, trf2, stf2))

        w2_state = {"i": 0, "stf": 0}
        w2q_trf_tg = {}
        w2_ts_tg = {}
        w2_guard = {"pr": NONE}

        def _w2_stf_upto(lim):
            while w2_state["stf"] < min(lim, w2_state["i"]):
                idx = w2_state["stf"]
                qs = idx % 2
                j, _, stf = w2q_steps[idx]
                b.wait("gpsimd", s_q2r[qs], w2q_trf_tg[idx])
                b.dma("gpsimd", t_q2s[qs], lambda e, stf=stf, qs=qs: stf(e, qs))
                w2_ts_tg[j] = (t_q2s[qs], t_q2s[qs].n)
                w2_state["stf"] = idx + 1

        def emit_w2q(k):
            for _ in range(k):
                i = w2_state["i"]
                if i >= len(w2q_steps):
                    return
                if cast_done.get("w2") is None:
                    return
                _w2_stf_upto(i - 1)
                qs = i % 2
                trf = w2q_steps[i][1]
                b.waitp("sync", w2_guard["pr"])
                b.wait("sync", g_c["w2"], cast_done["w2"])
                b.wait("sync", t_q2s[qs], t_q2s[qs].n)
                w2q_trf_tg[i] = b.dma(
                    "sync", s_q2r[qs], lambda e, trf=trf, qs=qs: trf(e, qs))
                w2_state["i"] = i + 1

        prep_state = {"cast_i": 0}

        def emit_prep_casts(k):
            for _ in range(k):
                i = prep_state["cast_i"]
                if i >= len(prep_cast_steps):
                    return
                fn, nm = prep_cast_steps[i]
                v = b.dma("gpsimd", g_c[nm], fn)
                if g_c[nm].n == 256:
                    cast_done[nm] = v
                prep_state["cast_i"] = i + 1

        b.wait("vector", s_cn, consts_ld)
        b.emit("vector", lambda e: e.tensor_copy(
            fB[:, 0:128], i128[:]).then_inc(dve.h, 1))
        idf_dve = dve.inc()
        xbt_free = NONE
        p1_bank_free = [NONE, NONE]
        fa1_free = [NONE, NONE]
        for t in range(NC):
            slot = t % 2
            b.waitp("sync", fa1_free[slot])
            ld = b.dma("sync", s_fa[slot], lambda e, t=t, slot=slot: e.dma_start(
                fA[:, 2048 * slot:2048 * slot + 2048], x_d[128 * t:128 * t + 128, :]))
            if t < 8:
                emit_prep_casts(2)
            b.wait("tensor", s_fa[slot], ld)
            b.wait("tensor", s_cn, consts_ld)
            b.wait("tensor", dve, idf_dve)
            for q4 in range(4):
                tq = q4 % 2
                bank = 1024 + 512 * tq
                b.waitp("tensor", p1_bank_free[tq])
                for kq in range(4):
                    c = 4 * q4 + kq
                    last = (kq == 3)
                    def mmf(e, c=c, bank=bank, kq=kq, last=last, slot=slot):
                        mm = e.matmul(pbB[:, bank + 128 * kq:bank + 128 * kq + 128],
                                      fA[:, 2048 * slot + 128 * c:2048 * slot + 128 * c + 128],
                                      fB[:, 0:128], start=True, stop=True)
                        if last:
                            mm.then_inc(pe.h, 1)
                    b.emit("tensor", mmf)
                q_pe = pe.inc()
                b.wait("scalar", pe, q_pe)
                def evf(e, q4=q4, t=t, bank=bank):
                    dst = bigA[:].rearrange("p (c q) -> p c q", c=16)[
                        :, 4 * q4:4 * q4 + 4, 128 * t:128 * t + 128]
                    e.copy(dst, pbB[:, bank:bank + 512]
                           .rearrange("p (c q) -> p c q", c=4)).then_inc(act.h, 1)
                b.emit("scalar", evf)
                p1_bank_free[tq] = (act, act.inc())
            fa1_free[slot] = (pe, pe.n)
        xT_done_act = act.n

        wv_free = NONE
        btmp_free = [NONE] * 4
        v_bank_free = [NONE] * 4
        for nnc in range(4):
            b.waitp("gpsimd", wv_free)
            def wvf(e, nnc=nnc):
                src = wqkv_d[:, 2 * D + 512 * nnc:2 * D + 512 * nnc + 512]
                return e.dma_start(warena[:].rearrange("p (c n) -> p c n", c=16),
                                   src.rearrange("(c p) n -> p c n", p=128))
            ldv = b.dma("gpsimd", g_wv, wvf)
            emit_prep_casts(2)
            emit_w13q(6)
            b.wait("tensor", g_wv, ldv)
            b.wait("tensor", act, xT_done_act)
            for t in range(NC):
                bk = t % 4
                bank = 512 * bk
                b.waitp("tensor", v_bank_free[bk])
                for c in range(NC):
                    last = (c == NC - 1)
                    def mmf(e, t=t, c=c, bank=bank, last=last):
                        mm = e.matmul(pbA[:, bank:bank + 512],
                                      bigA[:, 2048 * c + 128 * t:2048 * c + 128 * t + 128],
                                      warena[:, 512 * c:512 * c + 512],
                                      start=(c == 0), stop=last)
                        if last:
                            mm.then_inc(pe.h, 1)
                    b.emit("tensor", mmf)
                v_pe = pe.inc()
                b.waitp("vector", btmp_free[bk])
                b.wait("vector", pe, v_pe)
                b.emit("vector", lambda e, bk=bk, bank=bank: e.tensor_copy(
                    btmp[:, 512 * bk:512 * bk + 512],
                    pbA[:, bank:bank + 512]).then_inc(dve.h, 1))
                ev_dve = dve.inc()
                v_bank_free[bk] = (dve, ev_dve)
                b.wait("gpsimd", dve, ev_dve)
                stv = b.dma("gpsimd", t_bt[bk], lambda e, t=t, nnc=nnc, bk=bk: e.dma_start(
                    vsc_d[t][:, 512 * nnc:512 * nnc + 512],
                    btmp[:, 512 * bk:512 * bk + 512]))
                btmp_free[bk] = (t_bt[bk], stv)
            wv_free = (pe, pe.n)
        pbA_free = (dve, dve.n)
        vsc_totals = [(s, s.n) for s in t_bt]

        wqk_free = [wv_free, wv_free]
        vsl_free = [NONE, NONE]
        outt_free = [NONE, NONE]
        pt_free = NONE
        vh_lds = {}
        ldqks = {}
        exp_free = [pbA_free] * 4
        pv_free = [NONE, NONE]
        sum_free = [NONE, NONE]
        qk_free = NONE
        kres_free = NONE

        for h in range(NH):
            wslot = (h // 2) % 2
            arena = warena if wslot == 0 else fAb
            if h % 2 == 0:
                hp = h // 2
                b.waitp("gpsimd", wqk_free[wslot])
                def wqf(e, hp=hp, arena=arena):
                    src = wqkv_d[:, 256 * hp:256 * hp + 256]
                    return e.dma_start(
                        arena[:, 0:4096].rearrange("p (c d) -> p c d", c=16),
                        src.rearrange("(c p) d -> p c d", p=128))
                def wkf(e, hp=hp, arena=arena):
                    src = wqkv_d[:, D + 256 * hp:D + 256 * hp + 256]
                    return e.dma_start(
                        arena[:, 4096:8192].rearrange("p (c d) -> p c d", c=16),
                        src.rearrange("(c p) d -> p c d", p=128))
                b.dma("gpsimd", g_qk[wslot], wqf)
                ldqks[h // 2] = b.dma("gpsimd", g_qk[wslot], wkf)
            ldqk = ldqks[h // 2]

            b.wait("tensor", g_qk[wslot], ldqk)
            for pr in exp_free + pv_free + sum_free + [kres_free]:
                b.waitp("tensor", pr)
            hoff = 128 * (h % 2)
            for dst_ps, woff in ((pbA, hoff), (pbB, 4096 + hoff)):
                for c in range(NC):
                    for nn in range(4):
                        last = (c == NC - 1)
                        def mmf(e, dst_ps=dst_ps, woff=woff, nn=nn, c=c, last=last,
                                arena=arena):
                            mm = e.matmul(dst_ps[:, 512 * nn:512 * nn + 512],
                                          arena[:, woff + 256 * c:woff + 256 * c + 128],
                                          bigA[:, 2048 * c + 512 * nn:2048 * c + 512 * nn + 512],
                                          start=(c == 0), stop=last)
                            if last and nn == 3:
                                mm.then_inc(pe.h, 1)
                        b.emit("tensor", mmf)
                pe.inc()
            qkv_mm_pe = pe.n
            b.waitp("scalar", qk_free)
            b.wait("scalar", pe, qkv_mm_pe)
            b.emit("scalar", lambda e: e.copy(qk[:, 0:2048], pbA[:, 0:2048]).then_inc(act.h, 1))
            act.inc()
            b.emit("scalar", lambda e: e.copy(qk[:, 2048:4096], pbB[:, 0:2048]).then_inc(act.h, 1))
            qk_ready = act.inc()
            if h % 2 == 1:
                wqk_free[wslot] = (pe, qkv_mm_pe)
            exp_free = [(act, qk_ready)] * 4
            pv_free = [(act, qk_ready), (act, qk_ready)]
            sum_free = [(act, qk_ready), (act, qk_ready)]

            vslot = h % 2

            def emit_vhead(hh):
                vs = hh % 2
                for s_, v_ in vsc_totals:
                    b.wait("sync", s_, v_)
                b.waitp("sync", vsl_free[vs])
                def vhf(e, hh=hh, vs=vs):
                    src = vsc_d[:, :, 128 * hh:128 * hh + 128].rearrange("j p d -> p j d")
                    return e.dma_start(vsl[:, 2048 * vs:2048 * vs + 2048]
                                       .rearrange("p (j d) -> p j d", j=16), src)
                vh_lds[hh] = b.dma("sync", s_vs[vs], vhf)

            if h == 0:
                emit_vhead(0)
            if h + 1 < NH:
                emit_vhead(h + 1)
            vh_ld = vh_lds[h]

            if h < 12:
                emit_prep_casts(2)
            emit_w13q(8 if h < 10 else 5)
            if h >= 12:
                emit_w2q(8)

            last_tt_dve = 0
            for qp in range(2):
                qcs = (2 * qp, 2 * qp + 1)
                njv = 4 * qcs[1] + 4
                b.wait("tensor", act, qk_ready)
                b.waitp("tensor", pt_free)
                b.wait("tensor", s_vs[vslot], vh_ld)
                b.waitp("scalar", pt_free)
                exp_acts = {}
                for j in range(njv):
                    live = [m for m in range(2) if j <= 4 * qcs[m] + 3]
                    for m in live:
                        qc = qcs[m]
                        sbank = (2 * j + m) % 4
                        b.waitp("tensor", exp_free[sbank])
                        def smf(e, j=j, qc=qc, sbank=sbank):
                            e.matmul(pbA[:, 512 * sbank:512 * sbank + 512],
                                     qk[:, 2048 + 128 * j:2048 + 128 * j + 128],
                                     qk[:, 512 * qc:512 * qc + 512],
                                     start=True, stop=True).then_inc(pe.h, 1)
                        b.emit("tensor", smf)
                        s_pe = pe.inc()
                        r = j - 4 * qc
                        if 0 <= r <= 3:
                            b.waitp("vector", exp_free[sbank])
                            b.wait("vector", pe, s_pe)
                            b.emit("vector", lambda e, sbank=sbank, r=r: e.tensor_add(
                                pbA[:, 512 * sbank:512 * sbank + 512],
                                pbA[:, 512 * sbank:512 * sbank + 512],
                                maskt[:, 512 * r:512 * r + 512]).then_inc(dve.h, 1))
                            mk = dve.inc()
                            b.wait("scalar", dve, mk)
                        else:
                            b.wait("scalar", pe, s_pe)
                        def expf(e, j=j, m=m, sbank=sbank):
                            e.activation(pt[:, 1024 * j + 512 * m:1024 * j + 512 * m + 512],
                                         pbA[:, 512 * sbank:512 * sbank + 512],
                                         AF.Exp, scale=SCALE).then_inc(act.h, 1)
                        b.emit("scalar", expf)
                        exp_free[sbank] = (act, act.inc())
                        exp_acts[(j, m)] = act.n
                for m in range(2):
                    b.waitp("tensor", pv_free[m])
                    b.waitp("tensor", sum_free[m])
                for j in range(njv):
                    for m in range(2):
                        if j > 4 * qcs[m] + 3:
                            continue
                        b.wait("tensor", act, exp_acts[(j, m)])
                        nj_m = 4 * qcs[m] + 4
                        def pvf(e, j=j, m=m, vslot=vslot, last=(j == 4 * qcs[m] + 3)):
                            mm = e.matmul(pbB[:, 512 * m:512 * m + 512],
                                          vsl[:, 2048 * vslot + 128 * j:2048 * vslot + 128 * j + 128],
                                          pt[:, 1024 * j + 512 * m:1024 * j + 512 * m + 512],
                                          start=(j == 0), stop=last)
                            if last:
                                mm.then_inc(pe.h, 1)
                        b.emit("tensor", pvf)
                        if j == 4 * qcs[m] + 3:
                            pv_pes = pe.inc()
                        def suf(e, j=j, m=m, last=(j == 4 * qcs[m] + 3)):
                            mm = e.matmul(pbB[:, 1024 + 512 * m:1536 + 512 * m],
                                          ones128[:],
                                          pt[:, 1024 * j + 512 * m:1024 * j + 512 * m + 512],
                                          start=(j == 0), stop=last)
                            if last:
                                mm.then_inc(pe.h, 1)
                        b.emit("tensor", suf)
                        if j == 4 * qcs[m] + 3:
                            sum_pes = pe.inc()
                            qc = qcs[m]
                            b.wait("vector", pe, sum_pes)
                            b.emit("vector", lambda e, m=m: e.reciprocal(
                                fB[:, 512 * m:512 * m + 512],
                                pbB[:, 1024 + 512 * m:1536 + 512 * m]).then_inc(dve.h, 1))
                            rc_dve = dve.inc()
                            sum_free[m] = (dve, rc_dve)
                            b.wait("vector", dve, rc_dve)
                            b.wait("vector", pe, pv_pes)
                            oslot = h % 2
                            if qc == 0:
                                b.waitp("vector", outt_free[oslot])
                            b.emit("vector", lambda e, m=m, qc=qc, oslot=oslot: e.tensor_mul(
                                outt[:, 2048 * oslot + 512 * qc:2048 * oslot + 512 * qc + 512],
                                pbB[:, 512 * m:512 * m + 512],
                                fB[:, 512 * m:512 * m + 512]).then_inc(dve.h, 1))
                            last_tt_dve = dve.inc()
                            pv_free[m] = (dve, last_tt_dve)
                pt_free = (pe, pe.n)
            b.wait("sync", dve, last_tt_dve)
            sto_t = b.dma("sync", t_ot[h % 2], lambda e, h=h, oslot=h % 2: e.dma_start(
                att_d[h], outt[:, 2048 * oslot:2048 * oslot + 2048]))
            outt_free[h % 2] = (t_ot[h % 2], sto_t)
            qk_free = (pe, pe.n)
            vsl_free[vslot] = (pe, pe.n)
            kres_free = (dve, last_tt_dve)

        pbA_free = (act, act.n)
        att_done_dve = dve.n
        att_pe_done = pe.n
        att_totals = [(s, s.n) for s in t_ot]
        emit_w13q(999)
        assert w13_state["i"] == len(w13q_steps), "w13 prep not fully emitted"
        _w13_stf(len(w13q_steps) - 2)
        _w13_stf(len(w13q_steps) - 1)
        w13_done = [(t_qst[0], t_qst[0].n), (t_qst[1], t_qst[1].n)]
        _w2_stf_upto(10**9)
        w2_sofar = [(t_q2s[0], t_q2s[0].n), (t_q2s[1], t_q2s[1].n)]
        assert prep_state["cast_i"] == len(prep_cast_steps)

        b.wait("gpsimd", pe, att_pe_done)
        for hh in range(NC):
            b.dma("gpsimd", g_wo, lambda e, hh=hh: e.dma_start(
                bigA[:, 2048 * hh:2048 * hh + 2048],
                wout_d[128 * hh:128 * hh + 128, :]))
        wout_ld = g_wo.n

        b.wait("tensor", s_cn, consts_ld)
        b.waitp("tensor", pbA_free)
        for nn in range(4):
            last = (nn == 3)
            def mmf(e, nn=nn, last=last):
                mm = e.matmul(pbA[:, 512 * nn:512 * nn + 512], onesf[:],
                              b_sb[:, 512 * nn:512 * nn + 512], start=True, stop=True)
                if last:
                    mm.then_inc(pe.h, 1)
            b.emit("tensor", mmf)
        bbc_pe = pe.inc()
        b.wait("vector", pe, bbc_pe)
        for s_, v_ in w2_sofar:
            b.wait("vector", s_, v_)
        b.emit("vector", lambda e: e.tensor_copy(b_bc[:], pbA[:, 0:2048]).then_inc(dve.h, 1))
        bbc_dve = dve.inc()

        fA_free = [wqk_free[1], wqk_free[1]]
        fB_free = [(dve, att_done_dve), NONE]
        proj_ps_free = [(dve, bbc_dve), (act, act.n)]
        proj_lds = {}

        def emit_proj_loads(t):
            tslot = t % 2
            for s_, v_ in att_totals:
                b.wait("sync", s_, v_)
            b.waitp("sync", vsl_free[tslot])
            def asl(e, t=t, tslot=tslot):
                src = att_d[:, :, 128 * t:128 * t + 128].rearrange("h p q -> p h q")
                return e.dma_start(vsl[:, 2048 * tslot:2048 * tslot + 2048]
                                   .rearrange("p (h q) -> p h q", h=16), src)
            asl_ld = b.dma("sync", s_vs[tslot], asl)
            b.waitp("sync", fA_free[tslot])
            xres_ld = b.dma("sync", s_fa[tslot], lambda e, t=t, tslot=tslot: e.dma_start(
                fA[:, 2048 * tslot:2048 * tslot + 2048],
                x_d[128 * t:128 * t + 128, :]))
            proj_lds[t] = (asl_ld, xres_ld)

        emit_proj_loads(0)
        for t in range(NC):
            tslot = t % 2
            PS = pbA if tslot == 0 else pbB
            if t + 1 < NC:
                emit_proj_loads(t + 1)
            asl_ld, xres_ld = proj_lds[t]

            b.wait("tensor", g_wo, wout_ld)
            b.wait("tensor", s_vs[tslot], asl_ld)
            b.waitp("tensor", proj_ps_free[tslot])
            for hh in range(NC):
                for nn in range(4):
                    last = (hh == NC - 1)
                    def mmf(e, PS=PS, tslot=tslot, hh=hh, nn=nn, last=last):
                        mm = e.matmul(PS[:, 512 * nn:512 * nn + 512],
                                      vsl[:, 2048 * tslot + 128 * hh:2048 * tslot + 128 * hh + 128],
                                      bigA[:, 2048 * hh + 512 * nn:2048 * hh + 512 * nn + 512],
                                      start=(hh == 0), stop=last)
                        if last and nn == 3:
                            mm.then_inc(pe.h, 1)
                    b.emit("tensor", mmf)
            proj_pe = pe.inc()
            vsl_free[tslot] = (pe, proj_pe)
            b.wait("vector", pe, proj_pe)
            b.wait("vector", s_fa[tslot], xres_ld)
            b.wait("vector", dve, bbc_dve)
            b.waitp("vector", fB_free[0])
            b.emit("vector", lambda e, PS=PS: e.tensor_add(
                fB[:, 0:2048], PS[:, 0:2048], b_bc[:]).then_inc(dve.h, 1))
            proj_ps_free[tslot] = (dve, dve.inc())
            b.wait("vector", dve, proj_ps_free[tslot][1])
            b.emit("vector", lambda e, tslot=tslot: e.tensor_add(
                fB[:, 0:2048], fB[:, 0:2048],
                fA[:, 2048 * tslot:2048 * tslot + 2048]).then_inc(dve.h, 1))
            dve.inc()
            fA_free[tslot] = (dve, dve.n)
            if debug:
                b.wait("sync", dve, dve.n)
                b.dma("sync", t_dbg, lambda e, t=t: e.dma_start(dbg_y1_d[t], fB[:, 0:2048]))
                b.wait("vector", t_dbg, t_dbg.n)
            emit_ln()
            b.wait("sync", dve, dve.n)
            b.dma("sync", t_fb, lambda e, t=t: e.dma_start(x1sc_d[t], fB[:, 0:2048]))
            fB_free[0] = (t_fb, t_fb.n)

        w2_guard["pr"] = (dve, dve.n)

        x1_store_done = t_fb.n
        hid_free = (pe, pe.n)
        barrier = (dve, dve.n)
        w13_free = [NONE, NONE]
        w2_free = [NONE] * 8
        sg_free = [NONE, NONE]
        gu_bank_free = [barrier] * 4
        sa_bank_free = [barrier, barrier]
        w2_ps_free = [barrier, barrier]
        x1t_free = NONE
        first_chunk = True

        def emit_x1t_build(kb):
            nonlocal x1t_free, first_chunk, xbt_free
            for tl in range(4):
                t = 4 * kb + tl
                aslot = tl % 2
                b.wait("sync", t_fb, x1_store_done)
                b.waitp("sync", fA_free[aslot])
                x1_ld = b.dma("sync", s_fa[aslot], lambda e, t=t, aslot=aslot: e.dma_start(
                    fA[:, 2048 * aslot:2048 * aslot + 2048], x1sc_d[t]))
                b.waitp("vector", xbt_free)
                b.wait("vector", s_fa[aslot], x1_ld)
                b.emit("vector", lambda e, aslot=aslot: e.tensor_copy(
                    xbt[:, 0:2048],
                    fA[:, 2048 * aslot:2048 * aslot + 2048]).then_inc(dve.h, 1))
                cv_dve = dve.inc()
                fA_free[aslot] = (dve, cv_dve)
                b.wait("tensor", dve, cv_dve)
                b.waitp("tensor", x1t_free)
                for q4 in range(4):
                    tq = q4 % 2
                    b.waitp("tensor", sa_bank_free[tq])
                    for kq in range(4):
                        c = 4 * q4 + kq
                        last = (kq == 3)
                        def mmf(e, c=c, tq=tq, kq=kq, last=last):
                            mm = e.matmul(pbB[:, 512 * tq + 128 * kq:512 * tq + 128 * kq + 128],
                                          xbt[:, 128 * c:128 * c + 128],
                                          i128[:], start=True, stop=True)
                            if last:
                                mm.then_inc(pe.h, 1)
                        b.emit("tensor", mmf)
                    tq_pe = pe.inc()
                    if first_chunk:
                        for s_, v_ in w13_done:
                            b.wait("scalar", s_, v_)
                        first_chunk = False
                    b.wait("scalar", pe, tq_pe)
                    def evf(e, q4=q4, tq=tq, tl=tl):
                        dst = bigB[:].rearrange("p (c x) -> p c x", c=16)[
                            :, 4 * q4:4 * q4 + 4, 128 * tl:128 * tl + 128]
                        e.copy(dst, pbB[:, 512 * tq:512 * tq + 512]
                               .rearrange("p (c x) -> p c x", c=4)).then_inc(act.h, 1)
                    b.emit("scalar", evf)
                    sa_bank_free[tq] = (act, act.inc())
                xbt_free = (pe, pe.n)

        emit_x1t_build(0)
        x1t_act = act.n
        bf_state = {"ct": 0, "slot": [0] * 4, "tile": [0] * NJ, "fin": []}

        w2_lds = {}
        NW2 = 8 * NJ

        def emit_w2_load(g):
            if g >= NW2:
                return
            slot = g % 8
            jt = W2_ORDER[g % NJ]
            q = "sync" if slot % 2 == 0 else "gpsimd"
            b.waitp(q, w2_ts_tg[jt])
            b.waitp(q, w2_free[slot])
            w2_lds[g] = b.dma(q, s_w2[slot], lambda e, j=jt, slot=slot:
                              e.dma_start(pt[:, 2048 * slot:2048 * slot + 2048],
                                          w2t_d[j]))

        w2_free = [(pe, pe.n)] * 8
        for gp in range(8):
            emit_w2_load(gp)

        for kk in range(4):
            bf_state["fin"] = []
            lvl = [7] * NJ
            pairs = []
            for s in range(5, -1, -1):
                stp = 1 << s
                for base in range(0, NJ, 2 * stp):
                    for o in range(stp):
                        pairs.append([s, base + o, base + o + stp, False])

            def emit_ready_pairs(cap=None):
                ct = 0
                progress = True
                while progress:
                    progress = False
                    for pr in pairs:
                        if cap is not None and ct >= cap:
                            return
                        s, a_, b_, done = pr
                        if done or lvl[a_] != s + 1 or lvl[b_] != s + 1:
                            continue
                        a0, b0 = 512 * a_, 512 * b_
                        fin = (s == 0)
                        bs = bf_state["ct"] % 4
                        bf_state["ct"] += 1
                        b.wait("vector", act, bf_state["slot"][bs])
                        b.wait("vector", act, max(bf_state["tile"][a_],
                                                  bf_state["tile"][b_]))
                        b.emit("vector", lambda e, a0=a0, b0=b0, bs=bs: e.tensor_sub(
                            btmp[:, 512 * bs:512 * bs + 512],
                            bigA[:, a0:a0 + 512],
                            bigA[:, b0:b0 + 512]).then_inc(dve.h, 1))
                        sub_dve = dve.inc()
                        def addf(e, a0=a0, b0=b0, fin=fin):
                            mm = e.tensor_add(bigA[:, a0:a0 + 512],
                                              bigA[:, a0:a0 + 512], bigA[:, b0:b0 + 512])
                            if fin:
                                mm.then_inc(dve.h, 1)
                        b.emit("vector", addf)
                        add_tg = dve.inc() if fin else None
                        b.wait("scalar", dve, sub_dve)
                        b.emit("scalar", lambda e, b0=b0, bs=bs: e.copy(
                            bigA[:, b0:b0 + 512],
                            btmp[:, 512 * bs:512 * bs + 512]).then_inc(act.h, 1))
                        cp_act = act.inc()
                        bf_state["slot"][bs] = cp_act
                        bf_state["tile"][b_] = cp_act
                        if fin:
                            bf_state["fin"].append((a_, add_tg, b_, cp_act))
                        pr[3] = True
                        lvl[a_] = s
                        lvl[b_] = s
                        ct += 1
                        progress = True

            mult_dves = {}
            for ji in range(NJ):
                if kk == 0:
                    emit_w2q(1)
                j = J_ARRIVE[ji]
                jslot = ji % 2
                for s_, v_ in w13_done:
                    b.wait("sync", s_, v_)
                b.waitp("sync", w13_free[jslot])
                b.dma("sync", s_w13[jslot], lambda e, j=j, jslot=jslot: e.dma_start(
                    warena[:, 4096 * jslot:4096 * jslot + 2048], w1t_d[j]))
                for s_, v_ in w13_done:
                    b.wait("gpsimd", s_, v_)
                b.waitp("gpsimd", w13_free[jslot])
                slab_ld = b.dma("gpsimd", s_w13[jslot], lambda e, j=j, jslot=jslot: e.dma_start(
                    warena[:, 4096 * jslot + 2048:4096 * jslot + 4096], w3t_d[j]))

                b.wait("tensor", s_w13[jslot], slab_ld)
                b.wait("tensor", act, x1t_act)
                for woff, bank in ((4096 * jslot, 512 * jslot),
                                   (4096 * jslot + 2048, 1024 + 512 * jslot)):
                    b.waitp("tensor", gu_bank_free[bank // 512])
                    for c in range(NC):
                        last = (c == NC - 1)
                        def mmf(e, woff=woff, bank=bank, c=c, last=last):
                            mm = e.matmul(pbA[:, bank:bank + 512],
                                          warena[:, woff + 128 * c:woff + 128 * c + 128],
                                          bigB[:, 512 * c:512 * c + 512],
                                          start=(c == 0), stop=last)
                            if last:
                                mm.then_inc(pe.h, 1)
                        b.emit("tensor", mmf)
                    pe.inc()
                gu_pe = pe.n
                w13_free[jslot] = (pe, gu_pe)
                sslot = ji % 2
                b.waitp("scalar", sg_free[sslot])
                b.wait("scalar", pe, gu_pe)
                b.emit("scalar", lambda e, sslot=sslot, jslot=jslot: e.activation(
                    sg[:, 512 * sslot:512 * sslot + 512],
                    pbA[:, 512 * jslot:512 * jslot + 512], AF.Silu).then_inc(act.h, 1))
                silu_act = act.inc()
                gu_bank_free[jslot] = (act, silu_act)
                if j == 0:
                    b.waitp("vector", hid_free)
                b.wait("vector", act, silu_act)
                b.emit("vector", lambda e, j=j, sslot=sslot, jslot=jslot: e.tensor_mul(
                    bigA[:, 512 * j:512 * j + 512],
                    pbA[:, 1024 + 512 * jslot:1024 + 512 * jslot + 512],
                    sg[:, 512 * sslot:512 * sslot + 512]).then_inc(dve.h, 1))
                mult_dve = dve.inc()
                gu_bank_free[2 + jslot] = (dve, mult_dve)
                sg_free[sslot] = (dve, mult_dve)
                mult_dves[j] = mult_dve
                jds_l = [J_ARRIVE[ji - 1]] if ji > 0 else []
                if ji == NJ - 1:
                    jds_l.append(J_ARRIVE[ji])
                for pos, jd in enumerate(jds_l):
                    jds = (ji - 1 + pos) % 2
                    b.wait("tensor", dve, mult_dves[jd])
                    b.waitp("tensor", sa_bank_free[jds])
                    def mmf(e, jd=jd, jds=jds):
                        e.matmul(pbB[:, 512 * jds:512 * jds + 512], h128[:],
                                 bigA[:, 512 * jd:512 * jd + 512],
                                 start=True, stop=True).then_inc(pe.h, 1)
                    b.emit("tensor", mmf)
                    sa_pe = pe.inc()
                    b.wait("vector", pe, sa_pe)
                    b.emit("vector", lambda e, jd=jd, jds=jds: e.tensor_scalar_mul(
                        bigA[:, 512 * jd:512 * jd + 512],
                        pbB[:, 512 * jds:512 * jds + 512], WHT_SCALE).then_inc(dve.h, 1))
                    sa_bank_free[jds] = (dve, dve.inc())
                    lvl[jd] = 6
                    emit_ready_pairs(cap=None if ji == NJ - 1 else 3)
            x1t_free = (pe, pe.n)
            assert all(pr[3] for pr in pairs)
            if kk == 0:
                emit_w2q(999)
                assert w2_state["i"] == len(w2q_steps)
                _w2_stf_upto(10**9)
                w2_all = [(t_q2s[0], t_q2s[0].n), (t_q2s[1], t_q2s[1].n)]
            order = []
            tile_wait = {}
            for a_, dtg, b_, atg in bf_state["fin"]:
                order.append(a_)
                tile_wait[a_] = (dve, dtg)
                order.append(b_)
                tile_wait[b_] = (act, atg)
            assert order == W2_ORDER

            hid_ready = dve.n
            hid_act = act.n
            if debug and kk == 1:
                b.wait("sync", dve, hid_ready)
                b.wait("sync", act, hid_act)
                b.dma("sync", t_dbg, lambda e: e.dma_start(dbg_hid_d, bigA[:]))
                b.wait("tensor", t_dbg, t_dbg.n)

            for tc2 in range(2):
                b.waitp("tensor", w2_ps_free[0])
                b.waitp("tensor", w2_ps_free[1])
                gbase = (kk * 2 + tc2) * NJ
                x1r_lds = {}

                def emit_x1r_loads():
                    for t2 in range(2):
                        tg = 4 * kk + 2 * tc2 + t2
                        b.wait("sync", t_fb, x1_store_done)
                        b.waitp("sync", fA_free[t2])
                        x1r_lds[t2] = b.dma("sync", s_fa[t2], lambda e, tg=tg, aslot=t2:
                                            e.dma_start(fA[:, 2048 * aslot:2048 * aslot + 2048],
                                                        x1sc_d[tg]))

                for idx in range(NJ):
                    if idx == 40:
                        emit_x1r_loads()
                    tile = W2_ORDER[idx]
                    wsem, wtg = tile_wait[tile]
                    b.wait("tensor", wsem, wtg)
                    j4 = (gbase + idx) % 8
                    b.wait("tensor", s_w2[j4], w2_lds[gbase + idx])
                    for t2 in range(2):
                        PS = pbA if t2 == 0 else pbB
                        tloc = 2 * tc2 + t2
                        for nn in range(4):
                            last_j = (idx == NJ - 1)
                            last_mm = (t2 == 1 and nn == 3)
                            def mmf(e, PS=PS, tile=tile, j4=j4, tloc=tloc, nn=nn,
                                    idx=idx, last_j=last_j, last_mm=last_mm):
                                mm = e.matmul(PS[:, 512 * nn:512 * nn + 512],
                                              bigA[:, 512 * tile + 128 * tloc:512 * tile + 128 * tloc + 128],
                                              pt[:, 2048 * j4 + 512 * nn:2048 * j4 + 512 * nn + 512],
                                              start=(idx == 0), stop=last_j)
                                if last_mm:
                                    mm.then_inc(pe.h, 1)
                            b.emit("tensor", mmf)
                    w2_free[j4] = (pe, pe.inc())
                    emit_w2_load(gbase + idx + 8)
                w2_pe = pe.n
                for t2 in range(2):
                    PS = pbA if t2 == 0 else pbB
                    dst = fB[:, 0:2048] if t2 == 0 else b_bc[:]
                    if kk == 0 and t2 == 1:
                        for s_, v_ in w2_all:
                            b.wait("vector", s_, v_)
                    b.wait("vector", pe, w2_pe)
                    b.wait("vector", s_fa[t2], x1r_lds[t2])
                    b.waitp("vector", fB_free[t2])
                    b.emit("vector", lambda e, PS=PS, dst=dst, t2=t2: e.tensor_add(
                        dst, PS[:, 0:2048],
                        fA[:, 2048 * t2:2048 * t2 + 2048]).then_inc(dve.h, 1))
                    w2_ps_free[t2] = (dve, dve.inc())
                    fA_free[t2] = (dve, dve.n)
                if tc2 == 1 and kk < 3:
                    sa_bank_free = [w2_ps_free[1], w2_ps_free[1]]
                    emit_x1t_build(kk + 1)
                    x1t_act = act.n
                for t2 in range(2):
                    tg = 4 * kk + 2 * tc2 + t2
                    dst = fB[:, 0:2048] if t2 == 0 else b_bc[:]
                    emit_ln(None if t2 == 0 else b_bc[:])
                    b.wait("sync", dve, dve.n)
                    b.dma("sync", t_fb, lambda e, tg=tg, dst=dst: e.dma_start(
                        y_d[128 * tg:128 * tg + 128, :], dst))
                    fB_free[t2] = (t_fb, t_fb.n)
            hid_free = (pe, pe.n)
            gu_bank_free = [w2_ps_free[0]] * 2 + [w2_ps_free[1]] * 2
            sa_bank_free = [w2_ps_free[1], w2_ps_free[1]]
            w13_free = [(pe, pe.n), (pe, pe.n)]

        b.emit("sync", lambda e, v=t_fb.n: e.wait_ge(t_fb.h, v))

        with nc.Block() as block:
            @block.sync
            def _(e):
                for fn in b.ops["sync"]:
                    fn(e)

            @block.gpsimd
            def _(e):
                for fn in b.ops["gpsimd"]:
                    fn(e)

            @block.tensor
            def _(e):
                for fn in b.ops["tensor"]:
                    fn(e)

            @block.scalar
            def _(e):
                for fn in b.ops["scalar"]:
                    fn(e)

            @block.vector
            def _(e):
                for fn in b.ops["vector"]:
                    fn(e)

    counts = {e: len(b.ops[e]) for e in ENGINES}
    return nc, counts


_CACHE = {}


def _get_program():
    if "nc" not in _CACHE:
        nc, counts = build()
        _CACHE["nc"] = nc
        _CACHE["counts"] = counts
    return _CACHE["nc"], _CACHE["counts"]


def _consts():
    i128 = np.eye(128, dtype=BF)
    idx = np.arange(128)
    pop = np.zeros((128, 128), dtype=np.int64)
    for bit in range(7):
        pop += ((idx[:, None] >> bit) & 1) * ((idx[None, :] >> bit) & 1)
    h128 = np.where(pop % 2 == 0, 1.0, -1.0).astype(BF)
    p = idx[:, None]
    f = np.arange(512)[None, :]
    regs = [np.where(128 * r + p > f, -1e30, 0.0) for r in range(4)]
    maskt = np.concatenate(regs, axis=1).astype(BF)
    onesb = np.ones((128, 128), dtype=BF)
    onesf = np.ones((1, 128), dtype=np.float32)
    return i128, h128, maskt, onesb, onesf


def kernel(L0_attn_x, L0_attn_w_qkv, L0_attn_w_out, L0_attn_b_out,
           L0_ffn_W1, L0_ffn_W2, L0_ffn_W3):
    nc, _ = _get_program()
    i128, h128, maskt, onesb, onesf = _consts()
    x = np.ascontiguousarray(np.asarray(L0_attn_x, dtype=np.float32))
    wqkv = np.ascontiguousarray(np.asarray(L0_attn_w_qkv, dtype=np.float32))
    wout = np.ascontiguousarray(np.asarray(L0_attn_w_out, dtype=np.float32))
    bout = np.ascontiguousarray(np.asarray(L0_attn_b_out, dtype=np.float32)).reshape(1, D)
    W1 = np.ascontiguousarray(np.asarray(L0_ffn_W1, dtype=np.float32))
    W2 = np.ascontiguousarray(np.asarray(L0_ffn_W2, dtype=np.float32))
    W3 = np.ascontiguousarray(np.asarray(L0_ffn_W3, dtype=np.float32))
    in_maps = []
    for c in range(8):
        in_maps.append({
            "x": np.ascontiguousarray(x[c]),
            "w_qkv": wqkv, "w_out": wout, "b_out": bout,
            "W1": W1, "W2": W2, "W3": W3,
            "i128": i128, "h128": h128, "maskt": maskt, "onesb": onesb,
            "onesf": onesf,
        })
    res = run_bass_kernel_spmd(nc, in_maps, core_ids=list(range(8)))
    out = np.stack([np.asarray(r["y"], dtype=np.float32) for r in res.results], axis=0)
    return out


if __name__ == "__main__":
    nc, counts = _get_program()
    print("instruction list sizes:", counts, "total:", sum(counts.values()))
